# version 16
# baseline (speedup 1.0000x reference)
import numpy as np
import concourse.bass as bass
import concourse.mybir as mybir
from concourse.bass_utils import run_bass_kernel_spmd
from contextlib import ExitStack

F32 = mybir.dt.float32
BF16 = mybir.dt.bfloat16
F16 = mybir.dt.float16
AF = mybir.ActivationFunctionType
ALU = mybir.AluOpType
AX = mybir.AxisListType
MAGIC = 12582912.0
EPS = 1e-5

T = 2048
D = 1024
NH = 16
NKC = 8
NT = 16
II = 2816
NI = 22


class Sem:
    def __init__(self, nc, es, name):
        self.h = es.enter_context(nc.semaphore(name))
        self.val = 0

    def tok(self):
        return (self, self.val)


class Prog:
    ENGS = ("sync", "gpsimd", "vector", "scalar", "tensor")

    def __init__(self, nc):
        self.nc = nc
        self.streams = {e: [] for e in self.ENGS}
        self.seen = {e: {} for e in self.ENGS}

    def w(self, eng, *toks):
        for t in toks:
            if t is None:
                continue
            s, v = t
            if v > 0 and self.seen[eng].get(id(s), -1) < v:
                self.seen[eng][id(s)] = v
                self.streams[eng].append(lambda e, s=s, v=v: e.wait_ge(s.h, v))

    def o(self, eng, name, *args, inc=None, incn=1, waits=(), **kw):
        self.w(eng, *waits)

        def f(e, name=name, args=args, kw=kw, inc=inc, incn=incn):
            ins = getattr(e, name)(*args, **kw)
            if inc is not None:
                ins.then_inc(inc.h, incn)

        self.streams[eng].append(f)
        if inc is not None:
            inc.val += incn
            return inc.tok()
        return None

    def dma(self, eng, out, in_, inc, waits=(), **kw):
        return self.o(eng, "dma_start", inc=inc, incn=16, waits=waits,
                      out=out, in_=in_, **kw)

    def run_block(self, name):
        with self.nc.Block(name) as block:
            for eng in self.ENGS:
                lst = self.streams[eng]
                if lst:
                    getattr(block, eng)(lambda e, lst=lst: [f(e) for f in lst])
        self.streams = {e: [] for e in self.ENGS}


def build_nc():
    nc = bass.Bass("TRN2", target_bir_lowering=False)

    x_in = nc.dram_tensor("x_in", [T, D], F32, kind="ExternalInput").ap()
    wq_d = nc.dram_tensor("wq", [D, D], F32, kind="ExternalInput").ap()
    wk_d = nc.dram_tensor("wk", [D, D], F32, kind="ExternalInput").ap()
    wv_d = nc.dram_tensor("wv", [D, D], F32, kind="ExternalInput").ap()
    wo_d = nc.dram_tensor("wo", [D, D], F32, kind="ExternalInput").ap()
    b_rows_d = nc.dram_tensor("b_rows", [1, 5 * D], F32, kind="ExternalInput").ap()
    wg_d = nc.dram_tensor("wg", [II, D], F32, kind="ExternalInput").ap()
    wu_d = nc.dram_tensor("wu", [II, D], F32, kind="ExternalInput").ap()
    wd_d = nc.dram_tensor("wd", [D, II], F32, kind="ExternalInput").ap()
    wm_d = nc.dram_tensor("wm", [D, 6 * D], F32, kind="ExternalInput").ap()
    clips_d = nc.dram_tensor("clips", [1, 2], F32, kind="ExternalInput").ap()
    out_d = nc.dram_tensor("out", [T, D], F32, kind="ExternalOutput").ap()

    v_dram = nc.dram_tensor("v_spill", [II, T], F16).ap()
    q_dram = nc.dram_tensor("q_spill", [128, 8, T], BF16).ap()
    h2_dram = nc.dram_tensor("h2_spill", [NT, 128, D], F32).ap()
    cc1_i = nc.dram_tensor("cc1_i", [1, 1], F32).ap()
    cc1_o = nc.dram_tensor("cc1_o", [1, 1], F32).ap()
    cc2_i = nc.dram_tensor("cc2_i", [1, 1], F32).ap()
    cc2_o = nc.dram_tensor("cc2_o", [1, 1], F32).ap()

    es0 = ExitStack()
    S = lambda n: Sem(nc, es0, n)
    d_x, d_a, d_b, d_wo, d_r = S("d_x"), S("d_a"), S("d_b"), S("d_wo"), S("d_r")
    d_q0, d_q1 = S("d_q0"), S("d_q1")
    d_wm0, d_wm1 = S("d_wm0"), S("d_wm1")
    d_f0, d_f1, d_wd = S("d_f0"), S("d_f1"), S("d_wd")
    d_sp, d_v0, d_v1, d_out, d_cc = S("d_sp"), S("d_v0"), S("d_v1"), S("d_out"), S("d_cc")
    d_h0, d_h1 = S("d_h0"), S("d_h1")
    d_wv0, d_wv1, d_u0, d_u1 = S("d_wv0"), S("d_wv1"), S("d_u0"), S("d_u1")
    pe, ve, se, ge, cc = S("pe"), S("ve"), S("se"), S("ge"), S("cc")

    sb = lambda st, n, shp, dt: st.enter_context(nc.sbuf_tensor(n, shp, dt))
    pt = lambda st, n, shp, dt: st.enter_context(nc.psum_tensor(n, shp, dt))

    x_tm = sb(es0, "x_tm", [128, NT, D], F32)
    ident_b = sb(es0, "ident_b", [128, 128], BF16)
    ident_f = sb(es0, "ident_f", [128, 128], F32)
    ones_f = sb(es0, "ones_f", [1, 128], F32)
    ga_bc = sb(es0, "ga_bc", [128, D], BF16)
    gabo_bc = sb(es0, "gabo_bc", [128, D], BF16)
    bv_bc = sb(es0, "bv_bc", [128, D], BF16)
    bqk_sb = sb(es0, "bqk_sb", [128, 16], F32)
    sa_f = sb(es0, "sa_f", [128, 8], F32)
    ca1_f = sb(es0, "ca1_f", [128, 8], F32)
    cb1_f = sb(es0, "cb1_f", [128, 8], F32)
    sb_f = sb(es0, "sb_f", [128, 8], F32)
    gb_f = sb(es0, "gb_f", [128, 8], F32)
    clip_sb = sb(es0, "clip_sb", [1, 2], F32)
    sc_sb = sb(es0, "sc_sb", [1, 4], F32)

    p = Prog(nc)

    def mk_stats(st, tag):
        return {
            "bn6": sb(st, "bn6" + tag, [128, 16, 12], F32),
            "mvb": sb(st, "mvb" + tag, [128, 16, 2], F32),
            "vrb": sb(st, "vrb" + tag, [128, 16], F32),
            "vinv": sb(st, "vinv" + tag, [128, 16], F32),
            "rstdb": sb(st, "rstdb" + tag, [128, 16], F32),
            "negmrb": sb(st, "negmrb" + tag, [128, 16], F32),
        }

    def ln_stats(sbuf, x_view, r0, ntl, waits=()):
        bn6, mvb = sbuf["bn6"], sbuf["mvb"]
        vrb, vinv = sbuf["vrb"], sbuf["vinv"]
        rstdb, negmrb = sbuf["rstdb"], sbuf["negmrb"]
        tk = None
        for tl in range(ntl):
            t = r0 + tl
            p.o("vector", "bn_stats", bn6[:, t, 0:6], x_view[:, t, 0:512],
                waits=waits if tl == 0 else ())
            tk = p.o("vector", "bn_stats", bn6[:, t, 6:12], x_view[:, t, 512:1024], inc=ve)
        p.w("vector", tk)
        for tl in range(ntl):
            t = r0 + tl
            tk = p.o("vector", "bn_aggr", mvb[:, t, :], bn6[:, t, :], inc=ve)
        p.w("vector", tk)
        tk = p.o("vector", "tensor_scalar", vrb[:, r0:r0 + ntl],
                 mvb[:, r0:r0 + ntl, 1], EPS, None, ALU.add, inc=ve)
        p.w("vector", tk)
        tk = p.o("vector", "reciprocal", vinv[:, r0:r0 + ntl], vrb[:, r0:r0 + ntl], inc=ve)
        t_sq = p.o("scalar", "activation", rstdb[:, r0:r0 + ntl], vinv[:, r0:r0 + ntl],
                   AF.Sqrt, waits=(tk,), inc=se)
        p.w("scalar", t_sq)
        t_nm = p.o("vector", "scalar_tensor_tensor", negmrb[:, r0:r0 + ntl],
                   mvb[:, r0:r0 + ntl, 0], -1.0, rstdb[:, r0:r0 + ntl],
                   ALU.mult, ALU.mult, waits=(t_sq,), inc=ve)
        return t_nm

    statsA = mk_stats(es0, "A")

    esS = ExitStack()
    wm_sb = sb(esS, "wm_sb", [128, 2, NKC, D], BF16)
    mod_sb = sb(esS, "mod_sb", [1, 6 * D], F32)
    rows = sb(esS, "rows", [1, 5 * D], F32)
    c_sb = sb(esS, "c_sb", [128, 8], F32)
    cs_sb = sb(esS, "cs_sb", [128, 8], BF16)
    gaf = sb(esS, "gaf", [1, 2, D], F32)
    rsc = sb(esS, "rsc", [1, 2, D], F32)
    ps_mod = pt(esS, "ps_mod", [1, 2, 512], F32)
    ps_bc = pt(esS, "ps_bc", [128, 2, 512], F32)
    ps_row = pt(esS, "ps_row", [128, 40], F32)

    x_r = x_in.rearrange("(n p) d -> p n d", p=128)
    t_xq = [p.dma("sync", x_tm[:, 4 * q:4 * q + 4, :], x_r[:, 4 * q:4 * q + 4, :],
                  (d_x, d_h0, d_h1, d_v0)[q]) for q in range(4)]
    t_rows0 = p.dma("sync", rows[:, :], b_rows_d, d_r)
    t_rows = p.dma("sync", clip_sb[:, :], clips_d, d_r, waits=(t_rows0,))

    wm_r = wm_d.rearrange("(c p) m -> p c m", p=128)
    t_wm = [None] * 6
    t_wm[0] = p.dma("gpsimd", wm_sb[:, 0, :, :], wm_r[:, :, 0:D], d_wm0)
    t_wm[1] = p.dma("gpsimd", wm_sb[:, 1, :, :], wm_r[:, :, D:2 * D], d_wm1)

    tgm = p.o("gpsimd", "memset", ident_b[:, :], 0.0, inc=ge)
    tgm2 = p.o("gpsimd", "memset", ident_f[:, :], 0.0, inc=ge)
    p.o("gpsimd", "affine_select", out=ident_b[:, :], in_=ident_b[:, :],
        compare_op=ALU.not_equal, fill=1.0, base=0,
        pattern=[[-1, 128]], channel_multiplier=1, waits=(tgm,))
    p.o("gpsimd", "affine_select", out=ident_f[:, :], in_=ident_f[:, :],
        compare_op=ALU.not_equal, fill=1.0, base=0,
        pattern=[[-1, 128]], channel_multiplier=1, waits=(tgm2,))
    p.o("gpsimd", "memset", ones_f[:, :], 1.0)
    t_ident = p.o("gpsimd", "memset", sc_sb[:, :], 0.0, inc=ge)

    t_rt = None
    for r in range(3):
        for ch in range(8):
            rr = r if r < 2 else 4
            t_rt = p.o("tensor", "transpose", ps_row[:, r * 8 + ch:r * 8 + ch + 1],
                       rows[:, rr * D + ch * 128: rr * D + (ch + 1) * 128],
                       ident_f[0:1, 0:1],
                       waits=(t_rows, t_ident) if (r == 0 and ch == 0) else (),
                       inc=pe if (r == 2 and ch == 7) else None)
    t_bqk = p.o("vector", "tensor_copy", bqk_sb[:, :], ps_row[:, 0:16], waits=(t_rt,), inc=ve)
    t_csb = p.o("vector", "tensor_copy", c_sb[:, :], ps_row[:, 16:24], inc=ve)
    t_cs = p.o("scalar", "activation", cs_sb[:, :], c_sb[:, :], AF.Silu,
               waits=(t_csb,), inc=se)

    t_mm = [None] * 12
    t_mev = [None] * 12
    for v in range(6):
        if v >= 2:
            sem_v = d_wm0 if v % 2 == 0 else d_wm1
            t_wm[v] = p.dma("gpsimd", wm_sb[:, v % 2, :, :],
                            wm_r[:, :, v * D:(v + 1) * D], sem_v,
                            waits=(t_mm[2 * (v - 2) + 1],))
        for jj in range(2):
            j = 2 * v + jj
            for kc_ in range(NKC):
                wl = [] if kc_ > 0 else [t_wm[v], t_cs] + ([t_mev[j - 2]] if j >= 2 else [])
                t_mm[j] = p.o("tensor", "matmul", ps_mod[:, j % 2, :],
                              cs_sb[:, kc_:kc_ + 1],
                              wm_sb[:, v % 2, kc_, jj * 512:jj * 512 + 512],
                              start=(kc_ == 0), stop=(kc_ == NKC - 1),
                              waits=tuple(wl), inc=pe if kc_ == NKC - 1 else None)
            t_mev[j] = p.o("vector", "tensor_copy", mod_sb[:, j * 512:(j + 1) * 512],
                           ps_mod[:, j % 2, :], waits=(t_mm[j],), inc=ve)

    t_gaf = [None, None]
    for gi, off in ((0, 2 * D), (1, 5 * D)):
        wl = (t_mev[off // 512], t_mev[off // 512 + 1])
        row = mod_sb[:, off:off + D]
        t_s1 = p.o("scalar", "activation", rsc[:, gi, :], row, AF.Sign, waits=wl, inc=se)
        t_s2 = p.o("scalar", "activation", gaf[:, gi, :], row, AF.Abs, inc=se)
        tmx_ = p.o("vector", "tensor_scalar", gaf[:, gi, :], gaf[:, gi, :], 0.25, None,
                   ALU.max, waits=(t_s1, t_s2), inc=ve)
        p.w("vector", tmx_)
        t_gaf[gi] = p.o("vector", "tensor_tensor", gaf[:, gi, :], gaf[:, gi, :],
                        rsc[:, gi, :], ALU.mult, inc=ve)

    t_rt2 = None
    vec_jobs = (("sa", mod_sb[:, 0:D], ()),
                ("ca", mod_sb[:, D:2 * D], ()),
                ("cb", mod_sb[:, 4 * D:5 * D], ()),
                ("sbv", mod_sb[:, 3 * D:4 * D], ()),
                ("gb", gaf[:, 1, :], (t_gaf[1],)))
    for vi, (nm, row, wl) in enumerate(vec_jobs):
        for ch in range(8):
            w0 = tuple(wl) + ((t_mev[11], t_bqk, t_csb) if (vi == 0 and ch == 0) else ())
            t_rt2 = p.o("tensor", "transpose", ps_row[:, vi * 8 + ch:vi * 8 + ch + 1],
                        row[:, ch * 128:(ch + 1) * 128],
                        ident_f[0:1, 0:1], waits=w0,
                        inc=pe if (vi == 4 and ch == 7) else None)
    p.o("vector", "tensor_copy", sa_f[:, :], ps_row[:, 0:8], waits=(t_rt2,))
    p.o("vector", "tensor_scalar", ca1_f[:, :], ps_row[:, 8:16], 1.0, None, ALU.add)
    p.o("vector", "tensor_scalar", cb1_f[:, :], ps_row[:, 16:24], 1.0, None, ALU.add)
    p.o("vector", "tensor_copy", sb_f[:, :], ps_row[:, 24:32])
    t_modf = p.o("vector", "tensor_copy", gb_f[:, :], ps_row[:, 32:40], inc=ve)

    bc_jobs = []
    bc_jobs.append(("ga", gaf[:, 0, :], ga_bc, None, (t_gaf[0],)))
    bc_jobs.append(("bv", rows[:, 2 * D:3 * D], bv_bc, None, (t_rows,)))
    bc_jobs.append(("bo", rows[:, 3 * D:4 * D], gabo_bc, "gabo", (t_rows,)))
    ps_free = [None, None]
    t_gabc = None
    for bi, (nm, row, outt, post, wl) in enumerate(bc_jobs):
        for half in range(2):
            slot = (2 * bi + half) % 2
            t_p = p.o("tensor", "matmul", ps_bc[:, slot, :], ones_f[:, :],
                      row[:, half * 512:half * 512 + 512], start=True, stop=True,
                      waits=tuple(wl) + ((ps_free[slot],) if ps_free[slot] else ()),
                      inc=pe)
            dst = outt[:, half * 512:half * 512 + 512]
            if post == "gabo":
                ps_free[slot] = p.o("vector", "tensor_tensor", dst, ps_bc[:, slot, :],
                                    ga_bc[:, half * 512:half * 512 + 512], ALU.mult,
                                    waits=(t_p, t_gabc), inc=ve)
            else:
                ps_free[slot] = p.o("vector", "tensor_copy", dst, ps_bc[:, slot, :],
                                    waits=(t_p,), inc=ve)
            if nm == "ga" and half == 1:
                t_gabc = ps_free[slot]
    t_setup_bc = ve.tok()

    t_nmA = [ln_stats(statsA, x_tm, 0, 8, waits=(t_xq[0], t_xq[1])),
             ln_stats(statsA, x_tm, 8, 8, waits=(t_xq[2], t_xq[3]))]
    p.run_block("setup")
    esS.close()

    esAB = ExitStack()
    q_cur = sb(esAB, "q_cur", [128, 2, NKC, 512], BF16)
    k_fm = sb(esAB, "k_fm", [128, NKC, T], BF16)
    v_tm = sb(esAB, "v_tm", [128, NT, NH, 65], BF16)

    esA = ExitStack()
    h1_half = sb(esA, "h1_half", [128, NKC, 1024], BF16)
    h1a = sb(esA, "h1a", [128, 2, D], F32)
    q_st = sb(esA, "q_st", [128, 2, 512], BF16)
    wqk_buf = sb(esA, "wqk_buf", [128, 2, NKC, 128], BF16)
    wv_buf = sb(esA, "wv_buf", [128, NKC, 512], BF16)
    ps_h1t = pt(esA, "ps_h1t", [128, 2, 1024], F32)
    ps_qkv = pt(esA, "ps_qkv", [128, 4, 512], F32)

    t_vones = p.o("gpsimd", "memset", v_tm[:, :, :, 64:65], 1.0, inc=ge)
    h1a_free = [None, None]
    qst_free = [None, None]
    psq_free = [None, None, None, None]
    psh_free = [None, None]
    wqk_free = [None, None]
    wq_r = wq_d.rearrange("(c p) m -> p c m", p=128)
    wk_r = wk_d.rearrange("(c p) m -> p c m", p=128)
    wv_r = wv_d.rearrange("(c p) m -> p c m", p=128)
    t_wv = None
    wv_free = [None, None]
    psq_i = 0

    for hf in range(2):
        r0 = hf * 8
        wq_r0 = wq_r if True else None
        pre_w = {}
        for mc in (0, 1):
            semw = d_a if mc == 0 else d_b
            pre_w[mc] = p.dma("gpsimd", wqk_buf[:, mc, :, :],
                              wq_r[:, :, mc * 128:(mc + 1) * 128], semw,
                              waits=(wqk_free[mc],))
            wqk_free[mc] = None
        t_nm = t_nmA[hf]
        t_h1ev = None
        for tl in range(8):
            t = r0 + tl
            par = tl % 2
            tA = p.o("scalar", "activation", h1a[:, par, :], x_tm[:, t, :], AF.Identity,
                     bias=statsA["negmrb"][:, t:t + 1], scale=statsA["rstdb"][:, t:t + 1],
                     waits=(t_nm, t_xq[2 * hf], t_xq[2 * hf + 1],
                            h1a_free[par]), inc=se)
            tD = None
            for ch in range(8):
                tD = p.o("tensor", "transpose", ps_h1t[:, par, ch * 128:(ch + 1) * 128],
                         h1a[:, par, ch * 128:(ch + 1) * 128], ident_f[:, :],
                         waits=(tA, psh_free[par]) if ch == 0 else (),
                         inc=pe if ch == 7 else None)
            h1a_free[par] = tD
            for ch in range(8):
                t_h1ev = p.o("vector", "tensor_scalar",
                             h1_half[:, ch, tl * 128:(tl + 1) * 128],
                             ps_h1t[:, par, ch * 128:(ch + 1) * 128],
                             ca1_f[:, ch:ch + 1], sa_f[:, ch:ch + 1],
                             ALU.mult, ALU.add,
                             waits=(tD, t_modf) if ch == 0 else (),
                             inc=ve if ch == 7 else None)
            psh_free[par] = t_h1ev

        for pi, (w_r, qk_out) in enumerate(((wq_r, None), (wk_r, k_fm))):
            for mc in range(8):
                slot = mc % 2
                semw = d_a if slot == 0 else d_b
                if pi == 0 and mc < 2:
                    t_w = pre_w[mc]
                else:
                    t_w = p.dma("gpsimd", wqk_buf[:, slot, :, :],
                                w_r[:, :, mc * 128:(mc + 1) * 128], semw,
                                waits=(wqk_free[slot],))
                wqk_free[slot] = None
                for g in range(2):
                    sl = psq_i % 4
                    psq_i += 1
                    tmm = None
                    for kc_ in range(NKC):
                        wl = (t_w, t_h1ev, psq_free[sl]) if kc_ == 0 else ()
                        tmm = p.o("tensor", "matmul", ps_qkv[:, sl, :],
                                  wqk_buf[:, slot, kc_, :],
                                  h1_half[:, kc_, g * 512:g * 512 + 512],
                                  start=(kc_ == 0), stop=(kc_ == NKC - 1),
                                  waits=wl, inc=pe if kc_ == NKC - 1 else None)
                    if pi == 0:
                        qsl = (mc * 2 + g) % 2
                        tev = p.o("vector", "tensor_scalar",
                                  q_st[:, qsl, :],
                                  ps_qkv[:, sl, :], bqk_sb[:, mc:mc + 1],
                                  None, ALU.add,
                                  waits=(tmm, t_bqk, qst_free[qsl]), inc=ve)
                        qst_free[qsl] = p.dma(
                            "sync", q_dram[:, mc, hf * 1024 + g * 512:
                                           hf * 1024 + g * 512 + 512],
                            q_st[:, qsl, :], d_h0 if qsl == 0 else d_h1,
                            waits=(tev,))
                    else:
                        tev = p.o("vector", "tensor_scalar",
                                  qk_out[:, mc, hf * 1024 + g * 512: hf * 1024 + g * 512 + 512],
                                  ps_qkv[:, sl, :], bqk_sb[:, pi * 8 + mc:pi * 8 + mc + 1],
                                  None, ALU.add, waits=(tmm, t_bqk), inc=ve)
                    psq_free[sl] = tev
                    if g == 1:
                        wqk_free[slot] = tev
        for vh in range(2):
            t_wv = p.dma("gpsimd", wv_buf[:, :, :], wv_r[:, :, vh * 512:vh * 512 + 512],
                         d_wv0 if vh == 0 else d_wv1,
                         waits=(wv_free[vh], wv_free[1 - vh]))
            for tl in range(8):
                t = r0 + tl
                sl = psq_i % 4
                psq_i += 1
                tmm = None
                for kc_ in range(NKC):
                    wl = (t_wv, t_h1ev, psq_free[sl]) if kc_ == 0 else ()
                    tmm = p.o("tensor", "matmul", ps_qkv[:, sl, :],
                              h1_half[:, kc_, tl * 128:(tl + 1) * 128],
                              wv_buf[:, kc_, :],
                              start=(kc_ == 0), stop=(kc_ == NKC - 1),
                              waits=wl, inc=pe if kc_ == NKC - 1 else None)
                tev = p.o("vector", "tensor_tensor",
                          v_tm[:, t, vh * 8:(vh + 1) * 8, 0:64],
                          ps_qkv[:, sl, :].rearrange("p (h e) -> p h e", h=8),
                          bv_bc[:, vh * 512:vh * 512 + 512].rearrange("p (h e) -> p h e", h=8),
                          ALU.add, waits=(tmm, t_setup_bc, t_vones), inc=ve)
                psq_free[sl] = tev
                if tl == 7:
                    wv_free[vh] = tev

    p.run_block("phaseA")
    esA.close()


    esB = ExitStack()
    wo_sb = sb(esB, "wo_sb", [128, NKC, D], BF16)
    P_sb = sb(esB, "P_sb", [128, 28, 512], BF16)
    stg_cfm = sb(esB, "stg_cfm", [128, 1024], BF16)
    stage = stg_cfm.ap()[0:65, :].rearrange("p (a b) -> p a b", a=2)
    ctx_fm = stg_cfm.ap().rearrange("p (c f) -> p c f", c=8)
    ctxt = sb(esB, "ctxt", [128, 4, NH, 64], BF16)
    rden = sb(esB, "rden", [128, 2, 4], F32)
    denm = sb(esB, "denm", [128, 2, 4], F32)
    ps_s = pt(esB, "ps_s", [128, 4, 512], F32)
    ps_av = pt(esB, "ps_av", [128, 2, 512], F32)
    ps_t0 = pt(esB, "ps_t0", [128, 1024], BF16)
    ps_t1 = pt(esB, "ps_t1", [128, 1024], BF16)
    ps_t = (ps_t0, ps_t1)

    t_wo = p.dma("gpsimd", wo_sb[:, :, :], wo_d.rearrange("(c p) m -> p c m", p=128), d_wo)
    t_wo2 = None
    for ch in range(NKC):
        t_wo2 = p.o("vector", "tensor_tensor", wo_sb[:, ch, :], wo_sb[:, ch, :],
                    ga_bc[:, :], ALU.mult,
                    waits=(t_wo, t_setup_bc) if ch == 0 else (),
                    inc=ve if ch == NKC - 1 else None)

    sbank_free = [None] * 4
    ppair_free = [None] * 14
    qcur_free = [None, None]
    t_qld = [None, None]
    av_free = [None, None]
    stage_free = [None, None]
    ta_free = [None, None]
    tb_free = [None, None]
    cfm_free = None
    po_free = None
    tb_last = None
    gsi = 0
    gp = 0
    norm_tok = [None] * NH
    pre_tok = [None] * 4

    t_qld[0] = p.dma("sync", q_cur[:, 0, :, :], q_dram[:, :, 0:512], d_q0,
                     waits=(d_h0.tok(), d_h1.tok()))
    for qc in range(4):
        pre_tok[qc] = [None] * 4
        for j in range(4):
            pre_tok[qc][j] = p.o("gpsimd", "tensor_tensor", x_tm[:, qc * 4 + j, :],
                                 x_tm[:, qc * 4 + j, :], gabo_bc[:, :], ALU.add,
                                 waits=(t_setup_bc,) if (qc == 0 and j == 0) else (),
                                 inc=ge)

        exp_toks = [[None] * 8 for _ in range(NH)]
        exp_slot = [[0] * 8 for _ in range(NH)]
        av_tok = [None] * NH
        ta_tok = [None] * NH
        ev1_tok = [None] * NH

        def tb_task(j, qcb):
            nonlocal tb_last
            bank = ps_t[j % 2]
            ttb = None
            for ch in range(8):
                ttb = p.o("tensor", "transpose",
                          bank[:, ch * 128:(ch + 1) * 128],
                          ctxt[:, j, :, :].rearrange("p h e -> p (h e)")[:, ch * 128:(ch + 1) * 128],
                          ident_b[:, :],
                          waits=(last_norm, tb_free[j % 2]) if ch == 0 else (),
                          inc=pe if ch == 7 else None)
            tcf = p.o("vector", "tensor_copy", ctx_fm[:, :, :],
                      bank.ap().rearrange("p (c f) -> p c f", c=8),
                      waits=(ttb, cfm_free), inc=ve)
            tb_free[j % 2] = tcf
            ta_free[j % 2] = tcf
            if j == 3:
                tb_last = ttb
            return tcf

        def o_task(j, half, tcf, qcb):
            nonlocal cfm_free, po_free
            tom = None
            for kc_ in range(NKC):
                wl = (tcf, t_wo2, po_free, sbank_free[0]) if kc_ == 0 else ()
                tom = p.o("tensor", "matmul", ps_s[:, 0, :],
                          ctx_fm[:, kc_, :],
                          wo_sb[:, kc_, half * 512:half * 512 + 512],
                          start=(kc_ == 0), stop=(kc_ == NKC - 1),
                          waits=tuple(w for w in wl if w) if kc_ == 0 else (),
                          inc=pe if kc_ == NKC - 1 else None)
            if half == 1:
                cfm_free = tom
            po_free = p.o("vector", "tensor_tensor",
                          x_tm[:, qcb * 4 + j, half * 512:half * 512 + 512],
                          x_tm[:, qcb * 4 + j, half * 512:half * 512 + 512],
                          ps_s[:, 0, :], ALU.add,
                          waits=(tom, pre_tok[qcb][j]), inc=ve)
            sbank_free[0] = po_free

        def emit_exp(i, h, q_, tmm):
            nonlocal gp
            slot = gp % 14
            bank = 2 * (h % 2)
            texp = p.o("scalar", "activation",
                       P_sb[:, 2 * slot:2 * slot + 2, :].rearrange("p a b -> p (a b)"),
                       ps_s[:, bank:bank + 2, :].rearrange("p a b -> p (a b)"),
                       AF.Exp, scale=0.125,
                       waits=(tmm, ppair_free[slot]), inc=se)
            sbank_free[bank] = texp
            sbank_free[bank + 1] = texp
            exp_toks[h][q_] = texp
            exp_slot[h][q_] = slot
            gp += 1

        def s_half(i, half):
            h0, h1 = 2 * i, 2 * i + 1
            defer = []
            tmm = None
            for g in range(4 * half, 4 * half + 4):
                kc0 = 2 * g
                for h, kc_ in ((h0, kc0), (h0, kc0 + 1), (h1, kc0), (h1, kc0 + 1)):
                    bank = 2 * (h % 2) + kc_ % 2
                    base_p = (h % 2) * 64
                    wl = [sbank_free[bank]]
                    if i == 0 and g == 0:
                        wl.append(t_qld[qc % 2])
                    tmm = p.o("tensor", "matmul", ps_s[:, bank, :],
                              k_fm[base_p:base_p + 64, i, kc_ * 128:(kc_ + 1) * 128],
                              q_cur[base_p:base_p + 64, qc % 2, i, :],
                              start=True, stop=True,
                              waits=tuple(w for w in wl if w),
                              inc=pe if kc_ % 2 == 1 else None)
                    if kc_ % 2 == 1:
                        if g % 4 == 3:
                            defer.append((i, h, g, tmm))
                        else:
                            emit_exp(i, h, g, tmm)
            if i == NH // 2 - 1 and half == 1:
                qcur_free[qc % 2] = tmm
            return defer

        def av_half(i, half):
            h0, h1 = 2 * i, 2 * i + 1
            for q4 in range(2 * half, 2 * half + 2):
                for hp, h in ((0, h0), (1, h1)):
                    avl = None
                    for kav in range(4 * q4, 4 * q4 + 4):
                        q_ = kav // 2
                        slot = exp_slot[h][q_]
                        wl2 = [exp_toks[h][q_]]
                        if kav == 0:
                            wl2.append(av_free[hp])
                        avl = p.o("tensor", "matmul", ps_av[0:65, hp, :],
                                  v_tm[:, kav, h, :],
                                  P_sb[:, 2 * slot + kav % 2, :],
                                  start=(kav == 0), stop=(kav == 15),
                                  waits=tuple(w for w in wl2 if w),
                                  inc=pe if kav % 2 == 1 else None)
                        if kav % 2 == 1:
                            ppair_free[slot] = avl
                    if q4 == 3:
                        av_tok[h] = avl

        def ev1_pair(i):
            h0, h1 = 2 * i, 2 * i + 1
            for hp, h in ((0, h0), (1, h1)):
                ev1_tok[h] = p.o("vector", "tensor_copy", stage[:, hp, :],
                                 ps_av[0:65, hp, :],
                                 waits=(av_tok[h], stage_free[hp]), inc=ve)
                av_free[hp] = ev1_tok[h]

        def ta_pair(i):
            h0, h1 = 2 * i, 2 * i + 1
            for hp, h in ((0, h0), (1, h1)):
                tl = None
                for jj in range(4):
                    tl = p.o("tensor", "transpose",
                             ps_t[hp][:, jj * 66:jj * 66 + 65],
                             stage[0:65, hp, jj * 128:(jj + 1) * 128],
                             ident_b[0:65, 0:65],
                             waits=(ev1_tok[h], ta_free[hp]) if jj == 0 else (),
                             inc=pe if jj == 3 else None)
                ta_tok[h] = tl
                stage_free[hp] = tl

        def ev2_pair(i):
            h0, h1 = 2 * i, 2 * i + 1
            for hp, h in ((0, h0), (1, h1)):
                wl = [ta_tok[h]]
                if h == 0:
                    wl.append(tb_last)
                tav = ps_t[hp][:, 0:264].rearrange("p (a b) -> p a b", b=66)
                p.o("vector", "tensor_copy", ctxt[:, :, h, :], tav[:, :, 0:64],
                    waits=tuple(w for w in wl if w))
                tdc = p.o("vector", "tensor_copy", denm[:, hp, :],
                          tav[:, :, 64], inc=ve)
                ta_free[hp] = tdc
            p.w("vector", ta_free[1])
            t_rc = p.o("vector", "reciprocal", rden[:, :, :], denm[:, :, :], inc=ve)
            p.w("vector", t_rc)
            for hp, h in ((0, h0), (1, h1)):
                norm_tok[h] = None
                for jn in range(4):
                    norm_tok[h] = p.o("vector", "tensor_scalar", ctxt[:, jn, h, :],
                                      ctxt[:, jn, h, :], rden[:, hp, jn:jn + 1],
                                      None, ALU.mult, inc=ve if jn == 3 else None)

        NPAIR = NH // 2
        if qc + 1 < 4:
            t_qld[(qc + 1) % 2] = p.dma(
                "sync", q_cur[:, (qc + 1) % 2, :, :],
                q_dram[:, :, (qc + 1) * 512:(qc + 2) * 512],
                d_q0 if (qc + 1) % 2 == 0 else d_q1,
                waits=(qcur_free[(qc + 1) % 2],))
        for i in range(NPAIR):
            d3 = s_half(i, 0)
            if i > 0:
                av_half(i - 1, 1)
            for args in d3:
                emit_exp(*args)
            if i > 0:
                ev1_pair(i - 1)
            d7 = s_half(i, 1)
            if i > 0:
                ta_pair(i - 1)
            av_half(i, 0)
            for args in d7:
                emit_exp(*args)
            if i > 0:
                ev2_pair(i - 1)
        av_half(NPAIR - 1, 1)
        ev1_pair(NPAIR - 1)
        ta_pair(NPAIR - 1)
        ev2_pair(NPAIR - 1)
        last_norm = norm_tok[NH - 1]
        for jj in range(4):
            tcf = tb_task(jj, qc)
            o_task(jj, 0, tcf, qc)
            o_task(jj, 1, tcf, qc)
        stage_free[0] = cfm_free
        stage_free[1] = cfm_free

    p.run_block("phaseB")
    esB.close()
    esAB.close()

    esC = ExitStack()
    nwdT = sb(esC, "nwdT", [128, NI, D], BF16)
    G2_bc = sb(esC, "G2_bc", [128, D], F32)
    swd_all = sb(esC, "swd_all", [128, 8], F32)
    statsC = mk_stats(esC, "C")
    magic_col = sb(esC, "magic_col", [128, 1], F32)
    gmax1 = sb(esC, "gmax1", [1, 1], F32)
    gmax2 = sb(esC, "gmax2", [1, 1], F32)
    srow = sb(esC, "srow", [1, 4], F32)
    rsx1 = sb(esC, "rsx1", [128, 2], F32)
    rsx2 = sb(esC, "rsx2", [128, 2], F32)
    hred = sb(esC, "hred", [128, 24], F32)
    hacc = sb(esC, "hacc", [128, 1], F32)
    hrow = sb(esC, "hrow", [1, 128], F32)
    hl = sb(esC, "hl", [1, 1], F32)
    sig_sb = sb(esC, "sig_sb", [1, 2], F32)
    esPM = ExitStack()
    xn_fm = sb(esPM, "xn_fm", [128, NKC, T], BF16)
    ps_m = pt(esPM, "ps_m", [128, 512], F32)
    ps_r2 = pt(esPM, "ps_r2", [1, 1024], F32)
    esH = ExitStack()
    h2a = sb(esH, "h2a", [128, 2, D], F32)
    h2blk = sb(esH, "h2blk", [128, 2, D], F32)
    h2buf = sb(esH, "h2buf", [128, 2, 2 * D], F32)
    wd_nat = sb(esH, "wd_nat", [128, II], F32)
    nwd_nat = sb(esH, "nwd_nat", [128, II], BF16)
    mxw = sb(esH, "mxw", [128, 2, 2], F32)
    ps_h2 = pt(esH, "ps_h2", [128, 1024], F32)
    ps_tw = pt(esH, "ps_tw", [128, 1024], BF16)

    t_magic = p.o("gpsimd", "memset", magic_col[:, :], MAGIC, inc=ge)
    t_nm = ln_stats(statsC, x_tm, 0, 8)
    h2a_free = [None, None]
    h2blk_free = [None, None]
    psh2_free = None
    pstw_free = None
    t_swd = [None] * 8
    t_nwdT_done = None
    t_hm = None
    t_h2sp = [None] * NT
    for t in range(NT):
        if t == 8:
            t_nm = ln_stats(statsC, x_tm, 8, 8)
        par = t % 2
        tA = p.o("scalar", "activation", h2a[:, par, :], x_tm[:, t, :], AF.Identity,
                 bias=statsC["negmrb"][:, t:t + 1], scale=statsC["rstdb"][:, t:t + 1],
                 waits=(t_nm, h2a_free[par]), inc=se)
        tD = None
        for ch in range(8):
            tD = p.o("tensor", "transpose", ps_h2[:, ch * 128:(ch + 1) * 128],
                     h2a[:, par, ch * 128:(ch + 1) * 128], ident_f[:, :],
                     waits=(tA, psh2_free) if ch == 0 else (),
                     inc=pe if ch == 7 else None)
        h2a_free[par] = tD
        tev = None
        for ch in range(8):
            tev = p.o("vector", "tensor_scalar",
                      h2blk[:, par, ch * 128:(ch + 1) * 128],
                      ps_h2[:, ch * 128:(ch + 1) * 128],
                      cb1_f[:, ch:ch + 1], sb_f[:, ch:ch + 1],
                      ALU.mult, ALU.add,
                      waits=(tD, t_modf, h2blk_free[par]) if ch == 0 else (),
                      inc=ve if ch == 7 else None)
        psh2_free = tev
        p.w("vector", tev)
        t_hm = p.o("vector", "tensor_reduce", hred[:, t:t + 1], h2blk[:, par, :],
                   AX.X, ALU.max, apply_absolute_value=True, inc=ve)
        t_h2sp[t] = p.dma("sync", h2_dram[t, :, :], h2blk[:, par, :],
                          d_h0 if par == 0 else d_h1, waits=(t_hm,))
        h2blk_free[par] = t_h2sp[t]
    p.w("vector", t_hm)
    t_hm = p.o("vector", "tensor_reduce", hacc[:, :], hred[:, 0:NT], AX.X, ALU.max,
               apply_absolute_value=True, inc=ve)
    t_ht = p.o("tensor", "transpose", ps_r2[:, 0:128], hacc[:, :], ident_f[:, :],
               waits=(t_hm,), inc=pe)
    t_hr = p.o("vector", "tensor_copy", hrow[:, :], ps_r2[:, 0:128], waits=(t_ht,), inc=ve)
    p.w("vector", t_hr)
    t_hl = p.o("vector", "tensor_reduce", hl[:, :], hrow[:, :], AX.X, ALU.max, inc=ve)

    t_d1 = p.dma("gpsimd", cc1_i, hl[:, :], d_cc, waits=(t_hl,))
    t_cc1 = p.o("gpsimd", "collective_compute", "AllReduce", ALU.max,
                replica_groups=[list(range(8))], ins=[cc1_i], outs=[cc1_o],
                waits=(t_d1,), inc=cc)
    t_g1 = p.dma("gpsimd", gmax1[:, :], cc1_o, d_cc, waits=(t_cc1,))

    wd_state = {"free": None, "act": None, "nfree": None, "pstw": pstw_free,
                "done": None}

    def wd_chunk(hc):
        t_wdc = p.dma("sync", wd_nat[:, :], wd_d[hc * 128:(hc + 1) * 128, :],
                      d_wd, waits=(wd_state["free"],))
        tmx = p.o("vector", "tensor_reduce", mxw[:, 0, 0:1], wd_nat[:, :],
                  AX.X, ALU.max, apply_absolute_value=True, waits=(t_wdc,), inc=ve)
        p.w("vector", tmx)
        t_swd[hc] = p.o("vector", "tensor_scalar", swd_all[:, hc:hc + 1],
                        mxw[:, 0, 0:1], 1.0 / 127.0, None, ALU.mult, inc=ve)
        p.w("vector", t_swd[hc])
        trc = p.o("vector", "reciprocal", mxw[:, 0, 1:2], swd_all[:, hc:hc + 1],
                  waits=(wd_state["act"],), inc=ve)
        wd_state["act"] = p.o("scalar", "activation", wd_nat[:, :], wd_nat[:, :],
                              AF.Identity, scale=mxw[:, 0, 1:2],
                              bias=magic_col[:, 0:1],
                              waits=(trc, t_magic), inc=se)
        tqd = p.o("vector", "tensor_scalar", nwd_nat[:, :], wd_nat[:, :],
                  MAGIC, None, ALU.subtract,
                  waits=(wd_state["act"], wd_state["nfree"]), inc=ve)
        wd_state["free"] = tqd
        for b0 in range(0, NI, 8):
            nb = min(8, NI - b0)
            ttr = None
            for bi in range(nb):
                i = b0 + bi
                ttr = p.o("tensor", "transpose", ps_tw[:, bi * 128:(bi + 1) * 128],
                          nwd_nat[:, i * 128:(i + 1) * 128], ident_b[:, :],
                          waits=(tqd, wd_state["pstw"]) if bi == 0 else (),
                          inc=pe if bi == nb - 1 else None)
            wd_state["pstw"] = p.o("scalar", "activation",
                                   nwdT[:, b0:b0 + nb, hc * 128:(hc + 1) * 128],
                                   ps_tw.ap().rearrange("p (c f) -> p c f", c=8)[:, 0:nb, :],
                                   AF.Identity, waits=(ttr,), inc=se)
            wd_state["nfree"] = ttr
        wd_state["done"] = wd_state["pstw"]

    for hc in range(3):
        wd_chunk(hc)

    t_sg1 = p.o("scalar", "activation", sig_sb[:, 0:1], clip_sb[:, 0:1], AF.Sigmoid,
                waits=(t_rows,), inc=se)
    t_sg2 = p.o("scalar", "activation", sig_sb[:, 1:2], clip_sb[:, 1:2], AF.Sigmoid,
                inc=se)
    tq = p.o("vector", "tensor_tensor", sc_sb[:, 0:1], gmax1[:, :], sig_sb[:, 0:1],
             ALU.mult, waits=(t_g1, t_sg1), inc=ve)
    p.w("vector", tq)
    tq = p.o("vector", "reciprocal", sc_sb[:, 1:2], sc_sb[:, 0:1], inc=ve)
    p.w("vector", tq)
    p.o("vector", "tensor_scalar", srow[:, 0:1], sc_sb[:, 1:2], 127.0, None, ALU.mult)
    tq = p.o("vector", "tensor_scalar", srow[:, 1:2], sc_sb[:, 0:1], 1.0 / 127.0, None,
             ALU.mult, inc=ve)
    p.w("vector", tq)
    t_bq = p.o("tensor", "matmul", ps_m[:, 0:2], ones_f[:, :], srow[0:1, 0:2],
               start=True, stop=True, waits=(tq,), inc=pe)
    t_rsx1 = p.o("vector", "tensor_copy", rsx1[:, :], ps_m[:, 0:2], waits=(t_bq,), inc=ve)
    p.w("vector", t_rsx1)

    h2r = h2_dram.rearrange("t p (c f) -> p c t f", c=8)
    t_spall = (d_h0, d_h0.val)
    t_spall1 = (d_h1, d_h1.val)
    h2b_free = [None, None]
    t_xn_par = [None, None]
    t_ld = [None, None]
    t_ld[0] = p.dma("sync", h2buf[:, 0, :].rearrange("p (t f) -> p t f", t=NT),
                    h2r[:, 0, :, :], d_h0, waits=(t_spall, t_spall1))
    t_ld[1] = p.dma("sync", h2buf[:, 1, :].rearrange("p (t f) -> p t f", t=NT),
                    h2r[:, 1, :, :], d_h1)
    t_xnev = None
    for c0 in range(0, 8, 2):
        if c0 >= 2:
            t_ld[0] = p.dma("sync", h2buf[:, 0, :].rearrange("p (t f) -> p t f", t=NT),
                            h2r[:, c0, :, :], d_h0, waits=(h2b_free[0],))
            t_ld[1] = p.dma("sync", h2buf[:, 1, :].rearrange("p (t f) -> p t f", t=NT),
                            h2r[:, c0 + 1, :, :], d_h1, waits=(h2b_free[1],))
        for step in range(3):
            tstep = None
            for par in range(2):
                if step == 0:
                    tstep = p.o("vector", "tensor_scalar", h2buf[:, par, :],
                                h2buf[:, par, :], rsx1[:, 0:1], MAGIC, ALU.mult,
                                ALU.add, waits=(t_ld[par],), inc=ve if par else None)
                elif step == 1:
                    tstep = p.o("vector", "tensor_scalar", h2buf[:, par, :],
                                h2buf[:, par, :], MAGIC, 127.0, ALU.subtract, ALU.min,
                                inc=ve if par else None)
                else:
                    t_xnev = p.o("vector", "tensor_scalar", xn_fm[:, c0 + par, :],
                                 h2buf[:, par, :], -128.0, None, ALU.max, inc=ve)
                    h2b_free[par] = t_xnev
                    t_xn_par[par] = t_xnev
            if step < 2:
                p.w("vector", tstep)
        wd_chunk(3 + c0 // 2)
    wd_chunk(7)
    t_nwdT_done = wd_state["done"]
    esH.close()

    esL = ExitStack()
    rowC = sb(esL, "rowC", [1, D], F32)
    ps_t = pt(esL, "ps_t", [128, 1024], BF16)
    ps_g = pt(esL, "ps_g", [128, 2, 512], F32)
    ps_u = pt(esL, "ps_u", [128, 2, 512], F32)
    wg_nat = sb(esL, "wg_nat", [128, 2, D], F32)
    wu_nat = sb(esL, "wu_nat", [128, 2, D], F32)
    nw_nat = sb(esL, "nw_nat", [128, 2, 2, D], BF16)
    nwgT = sb(esL, "nwgT", [128, 2, NKC, 128], BF16)
    nwuT = sb(esL, "nwuT", [128, 2, NKC, 128], BF16)
    silu_sb = sb(esL, "silu_sb", [128, 2, 512], F32)
    v_sl = sb(esL, "v_sl", [128, T], F16)
    mxs = sb(esL, "mxs", [128, 4, 8], F32)

    wgn_free = [None, None]
    nwT_free = [[None, None], [None, None]]
    nwn_free = [[None, None], [None, None]]
    psg_free = [None, None]
    psu_free = [None, None]
    silu_free = [None, None]
    t_spill = None
    t_vmax = None
    t_sil_par = [None, None]
    t_v4_par = [None, None]
    pst_free = None
    t_f = [None, None]
    t_q = [[None, None], [None, None]]
    t_T = [[None, None], [None, None]]
    t_sv = [None, None]

    tg0 = p.dma("sync", wg_nat[:, 0, :], wg_d[0:128, :], d_f0,
                waits=(t_h2sp[NT - 2], t_h2sp[NT - 1], h2a_free[0], h2a_free[1]))
    tu0 = p.dma("sync", wu_nat[:, 0, :], wu_d[0:128, :], d_u0)
    t_f[0] = (tg0, tu0)
    tg1 = p.dma("sync", wg_nat[:, 1, :], wg_d[128:256, :], d_f1)
    tu1 = p.dma("sync", wu_nat[:, 1, :], wu_d[128:256, :], d_u1,
                waits=(t_xn_par[0],))
    t_f[1] = (tg1, tu1)

    def qd_stage(ic):
        par = ic % 2
        ql = ic % 4
        if ic >= 2:
            tgx = p.dma("sync", wg_nat[:, par, :], wg_d[ic * 128:(ic + 1) * 128, :],
                        d_f0 if par == 0 else d_f1, waits=(wgn_free[par],))
            tux = p.dma("sync", wu_nat[:, par, :], wu_d[ic * 128:(ic + 1) * 128, :],
                        d_u0 if par == 0 else d_u1)
            t_f[par] = (tgx, tux)
        wnats = (wg_nat, wu_nat)
        tl = None
        for wi in range(2):
            wl0 = (t_f[par] + (wd_state["nfree"],)) if (ic < 2 and wi == 0) else \
                  (t_f[par] if wi == 0 else ())
            tl = p.o("vector", "tensor_reduce", mxs[:, ql, wi:wi + 1],
                     wnats[wi][:, par, :], AX.X, ALU.max, apply_absolute_value=True,
                     waits=wl0, inc=ve if wi else None)
        p.w("vector", tl)
        for wi in range(2):
            tl = p.o("vector", "tensor_scalar", mxs[:, ql, wi:wi + 1],
                     mxs[:, ql, wi:wi + 1], 1.0 / 127.0, None, ALU.mult,
                     inc=ve if wi else None)
        p.w("vector", tl)
        for wi in range(2):
            tl = p.o("vector", "reciprocal", mxs[:, ql, 2 + wi:3 + wi],
                     mxs[:, ql, wi:wi + 1], inc=ve if wi else None)
        t_sv[par] = p.o("vector", "tensor_scalar", mxs[:, ql, 4:6],
                        mxs[:, ql, 0:2], rsx1[:, 1:2], None, ALU.mult,
                        waits=(t_sil_par[par], t_v4_par[par]), inc=ve)
        p.w("vector", t_sv[par])
        for wi in range(2):
            tl = p.o("vector", "tensor_scalar", wnats[wi][:, par, :],
                     wnats[wi][:, par, :], mxs[:, ql, 2 + wi:3 + wi], MAGIC,
                     ALU.mult, ALU.add, inc=ve if wi else None)
        p.w("vector", tl)
        for wi in range(2):
            t_q[par][wi] = p.o("vector", "tensor_scalar", nw_nat[:, par, wi, :],
                               wnats[wi][:, par, :], MAGIC, None, ALU.subtract,
                               waits=(nwn_free[par][wi],), inc=ve)
        wgn_free[par] = t_q[par][1]

    def qt_stage(ic):
        nonlocal pst_free
        par = ic % 2
        for wi, nwT in enumerate((nwgT, nwuT)):
            ttr = None
            for ch in range(8):
                ttr = p.o("tensor", "transpose", ps_t[:, ch * 128:(ch + 1) * 128],
                          nw_nat[:, par, wi, ch * 128:(ch + 1) * 128], ident_b[:, :],
                          waits=(t_q[par][wi], pst_free) if ch == 0 else (),
                          inc=pe if ch == 7 else None)
            nwn_free[par][wi] = ttr
            t_T[par][wi] = p.o("scalar", "activation", nwT[:, par, :, :],
                               ps_t.ap().rearrange("p (c f) -> p c f", c=8),
                               AF.Identity, waits=(ttr, nwT_free[par][wi]), inc=se)
            pst_free = t_T[par][wi]

    def mm_stage(ic, mid=None):
        nonlocal t_spill, t_vmax
        par = ic % 2
        ql = ic % 4
        tg = tu = t_v4 = tsil = None
        for tc in range(4):
            if tc == 2 and mid is not None:
                mid()
            gsl = tc % 2
            for kc_ in range(NKC):
                wl = (t_T[par][0], psg_free[gsl],
                      t_xnev if (ic == 0 and tc == 0) else None) if kc_ == 0 else ()
                tg = p.o("tensor", "matmul", ps_g[:, gsl, :],
                         nwgT[:, par, kc_, :], xn_fm[:, kc_, tc * 512:tc * 512 + 512],
                         start=(kc_ == 0), stop=(kc_ == NKC - 1),
                         waits=tuple(w for w in wl if w) if kc_ == 0 else (),
                         inc=pe if kc_ == NKC - 1 else None)
            tsil = p.o("scalar", "activation", silu_sb[:, gsl, :], ps_g[:, gsl, :],
                       AF.Silu, scale=mxs[:, ql, 4:5],
                       waits=(tg, t_sv[par], silu_free[gsl]), inc=se)
            psg_free[gsl] = tsil
            for kc_ in range(NKC):
                wl = (t_T[par][1], psu_free[gsl]) if kc_ == 0 else ()
                tu = p.o("tensor", "matmul", ps_u[:, gsl, :],
                         nwuT[:, par, kc_, :], xn_fm[:, kc_, tc * 512:tc * 512 + 512],
                         start=(kc_ == 0), stop=(kc_ == NKC - 1),
                         waits=wl, inc=pe if kc_ == NKC - 1 else None)
            t_v4 = p.o("vector", "scalar_tensor_tensor",
                       v_sl[:, tc * 512:tc * 512 + 512],
                       ps_u[:, gsl, :], mxs[:, ql, 5:6], silu_sb[:, gsl, :],
                       ALU.mult, ALU.mult,
                       waits=(tu, tsil, t_spill if tc == 0 else None,
                              t_vmax if tc == 0 else None), inc=ve)
            psu_free[gsl] = t_v4
            silu_free[gsl] = t_v4
        nwT_free[par] = [tg, tu]
        t_sil_par[par] = tsil
        t_v4_par[par] = t_v4
        t_vmax = p.o("vector", "tensor_reduce", hred[:, ic:ic + 1], v_sl[:, :],
                     AX.X, ALU.max, apply_absolute_value=True, waits=(t_v4,), inc=ve)
        t_spill = p.dma("sync", v_dram[ic * 128:(ic + 1) * 128, :], v_sl[:, :], d_sp,
                        waits=(t_v4,))

    qd_stage(0)
    qd_stage(1)
    qt_stage(0)
    for ic in range(NI):
        if ic + 2 < NI:
            qd_stage(ic + 2)
        mm_stage(ic, mid=(lambda ic=ic: qt_stage(ic + 1)) if ic + 1 < NI else None)

    p.w("vector", t_vmax)
    t_hm2 = p.o("vector", "tensor_reduce", hacc[:, :], hred[:, 0:NI], AX.X, ALU.max,
                apply_absolute_value=True, inc=ve)
    t_ht2 = p.o("tensor", "transpose", ps_r2[:, 0:128], hacc[:, :], ident_f[:, :],
                waits=(t_hm2,), inc=pe)
    t_hr2 = p.o("vector", "tensor_copy", hrow[:, :], ps_r2[:, 0:128], waits=(t_ht2,), inc=ve)
    p.w("vector", t_hr2)
    t_hl2 = p.o("vector", "tensor_reduce", hl[:, :], hrow[:, :], AX.X, ALU.max, inc=ve)
    t_d2 = p.dma("gpsimd", cc2_i, hl[:, :], d_cc, waits=(t_hl2,))
    t_cc2 = p.o("gpsimd", "collective_compute", "AllReduce", ALU.max,
                replica_groups=[list(range(8))], ins=[cc2_i], outs=[cc2_o],
                waits=(t_d2,), inc=cc)
    t_g2 = p.dma("gpsimd", gmax2[:, :], cc2_o, d_cc, waits=(t_cc2,))
    tq = p.o("vector", "tensor_tensor", sc_sb[:, 2:3], gmax2[:, :], sig_sb[:, 1:2],
             ALU.mult, waits=(t_g2, t_sg2), inc=ve)
    p.w("vector", tq)
    tq = p.o("vector", "reciprocal", sc_sb[:, 3:4], sc_sb[:, 2:3], inc=ve)
    p.w("vector", tq)
    p.o("vector", "tensor_scalar", srow[:, 2:3], sc_sb[:, 3:4], 127.0, None, ALU.mult)
    tq = p.o("vector", "tensor_scalar", srow[:, 3:4], sc_sb[:, 2:3], 1.0 / 127.0,
             None, ALU.mult, inc=ve)
    p.w("vector", tq)
    t_bq2 = p.o("tensor", "matmul", ps_m[:, 0:2], ones_f[:, :], srow[0:1, 2:4],
                start=True, stop=True, waits=(tq,), inc=pe)
    t_rsx2 = p.o("vector", "tensor_copy", rsx2[:, :], ps_m[:, 0:2], waits=(t_bq2,), inc=ve)
    p.w("vector", t_rsx2)
    t_swdg = p.o("vector", "tensor_tensor", swd_all[:, :], swd_all[:, :],
                 gb_f[:, :], ALU.mult, waits=(t_swd[7],), inc=ve)
    p.w("vector", t_swdg)
    tsr = None
    for hc in range(8):
        tsr = p.o("tensor", "transpose", ps_r2[:, hc * 128:(hc + 1) * 128],
                  swd_all[:, hc:hc + 1], ident_f[:, :],
                  waits=(t_swdg,) if hc == 0 else (),
                  inc=pe if hc == 7 else None)
    t_row = p.o("vector", "tensor_copy", rowC[:, :], ps_r2[:, 0:1024], waits=(tsr,), inc=ve)
    p.w("vector", t_row)
    t_G2 = None
    for half in range(2):
        tbb = p.o("tensor", "matmul", ps_m[:, :], ones_f[:, :],
                  rowC[0:1, half * 512:half * 512 + 512], start=True, stop=True,
                  waits=(t_row,) if half == 0 else (t_G2,), inc=pe)
        t_G2 = p.o("vector", "tensor_scalar", G2_bc[:, half * 512:half * 512 + 512],
                   ps_m[:, :], rsx2[:, 1:2], None,
                   ALU.mult, waits=(tbb,), inc=ve)

    p.run_block("phaseC")
    esL.close()
    esPM.close()

    esD = ExitStack()
    v_tc = sb(esD, "v_tc", [128, 2, NI, 512], F16)
    n2 = sb(esD, "n2", [128, 2, 512], BF16)
    tq1 = sb(esD, "tq1", [128, 2, 512], F32)
    tmpe = sb(esD, "tmpe", [128, 4, 512], F32)
    ps_d = pt(esD, "ps_d", [128, 8, 512], F32)

    v_r = v_dram.rearrange("(c p) t -> p c t", p=128)
    t_vld = [None] * 4
    vtc_free = [None, None]
    t_spill_all = d_sp.tok()

    def v_load(tc, waits=()):
        s0, s1 = (d_v0, d_v1) if tc % 2 == 0 else (d_f0, d_f1)
        ta_ = p.dma("sync", v_tc[:, tc % 2, 0:11, :],
                    v_r[:, 0:11, tc * 512:tc * 512 + 512], s0, waits=waits)
        tb_ = p.dma("sync", v_tc[:, tc % 2, 11:NI, :],
                    v_r[:, 11:NI, tc * 512:tc * 512 + 512], s1, waits=waits)
        return (ta_, tb_)

    t_vld[0] = v_load(0, waits=(t_spill_all,))
    t_vld[1] = v_load(1)

    n2_free = [None, None]
    tq_free = [None, None]
    tmpe_free = [None] * 4
    t_out = None
    for tc in range(4):
        par = tc % 2
        mm_tok = [None] * 8
        t_icmm = None
        t_n2p = [None, None]
        for ic0 in range(0, NI, 2):
            t_s0 = [None, None]
            for npar in range(2):
                ic = ic0 + npar
                t_s0[npar] = p.o("scalar", "activation", tq1[:, npar, :],
                                 v_tc[:, par, ic, :], AF.Identity,
                                 scale=rsx2[:, 0:1], bias=magic_col[:, 0:1],
                                 waits=t_vld[tc] + (t_rsx2, t_magic, tq_free[npar])
                                 if ic < 2 else (tq_free[npar],), inc=se)
            for step in range(2):
                tstep = None
                for npar in range(2):
                    if step == 0:
                        tstep = p.o("vector", "tensor_scalar", tq1[:, npar, :],
                                    tq1[:, npar, :], MAGIC, 127.0,
                                    ALU.subtract, ALU.min,
                                    waits=(t_s0[npar],), inc=ve if npar else None)
                    else:
                        t_n2p[npar] = p.o("vector", "tensor_scalar", n2[:, npar, :],
                                          tq1[:, npar, :], -128.0, None, ALU.max,
                                          waits=(n2_free[npar],), inc=ve)
                        tq_free[npar] = t_n2p[npar]
                if step < 1:
                    p.w("vector", tstep)
            for npar in range(2):
                ic = ic0 + npar
                t_n2 = t_n2p[npar]
                for tt in range(4):
                    for hh in range(2):
                        k = tt * 2 + hh
                        wl = []
                        if ic == 0:
                            wl = [t_n2, t_nwdT_done]
                            if tc >= 1:
                                wl.append(mm_free[k])
                        elif tt == 0 and hh == 0:
                            wl = [t_n2]
                        t_icmm = p.o("tensor", "matmul", ps_d[:, k, :],
                                     n2[:, npar, tt * 128:(tt + 1) * 128],
                                     nwdT[:, ic, hh * 512:hh * 512 + 512],
                                     start=(ic == 0), stop=(ic == NI - 1),
                                     waits=tuple(wl),
                                     inc=pe if (ic == NI - 1 or (tt == 3 and hh == 1)) else None)
                        if ic == NI - 1:
                            mm_tok[k] = t_icmm
                n2_free[npar] = t_icmm
                if ic == NI - 1:
                    vtc_free[par] = t_icmm
        if tc + 2 < 4:
            t_vld[tc + 2] = v_load(tc + 2, waits=(vtc_free[par],))
        mm_free = [None] * 8
        txa_tok = [None] * 8
        for grp in range(2):
            tmu_last = None
            for kk in range(4):
                k = grp * 4 + kk
                tt, hh = k // 2, k % 2
                tmu_last = p.o("vector", "tensor_tensor", tmpe[:, kk, :], ps_d[:, k, :],
                               G2_bc[:, hh * 512:hh * 512 + 512], ALU.mult,
                               waits=(mm_tok[k], t_G2, tmpe_free[kk]), inc=ve)
                mm_free[k] = tmu_last
            p.w("vector", tmu_last)
            txa_last = None
            for kk in range(4):
                k = grp * 4 + kk
                tt, hh = k // 2, k % 2
                txa_last = p.o("vector", "tensor_tensor",
                               x_tm[:, tc * 4 + tt, hh * 512:hh * 512 + 512],
                               x_tm[:, tc * 4 + tt, hh * 512:hh * 512 + 512],
                               tmpe[:, kk, :], ALU.add, inc=ve)
                tmpe_free[kk] = txa_last
                txa_tok[k] = txa_last
            p.w("vector", txa_last)
        for tt in range(4):
            is_last = (tc == 3 and tt == 3)
            t_out = p.dma("sync",
                          out_d.rearrange("(n p) d -> p n d", p=128)[:, tc * 4 + tt, :],
                          x_tm[:, tc * 4 + tt, :], d_out,
                          waits=(txa_tok[tt * 2 + 1],) + (((d_out, 240),) if is_last else ()))
    p.w("sync", d_out.tok())
    p.run_block("phaseD")
    esD.close()
    esC.close()
    es0.close()
    return nc


_NC_CACHE = None


def kernel(**inputs):
    global _NC_CACHE
    f32 = lambda k: np.ascontiguousarray(np.asarray(inputs[k], dtype=np.float32))
    x = f32("L0_attn_x")
    c = f32("L0_c")
    b_rows_base = np.stack([
        f32("L0_attn_bq").reshape(D), f32("L0_attn_bk").reshape(D),
        f32("L0_attn_bv").reshape(D), f32("L0_attn_bo").reshape(D),
        np.zeros(D, np.float32)], axis=0)
    clips = np.stack([f32("L0_ffn_clip_x").reshape(()),
                      f32("L0_ffn_clip_down").reshape(())]).reshape(1, 2)
    common = {
        "wq": f32("L0_attn_Wq"), "wk": f32("L0_attn_Wk"),
        "wv": f32("L0_attn_Wv"), "wo": f32("L0_attn_Wo"),
        "wg": f32("L0_ffn_w_gate"), "wu": f32("L0_ffn_w_up"),
        "wd": f32("L0_ffn_w_down"), "wm": f32("L0_wmod"),
        "clips": np.ascontiguousarray(clips),
    }
    if _NC_CACHE is None:
        _NC_CACHE = build_nc()
    nc = _NC_CACHE
    in_maps = []
    for b in range(8):
        rows_b = b_rows_base.copy()
        rows_b[4] = c[b]
        m = dict(common)
        m["x_in"] = np.ascontiguousarray(x[b])
        m["b_rows"] = np.ascontiguousarray(rows_b.reshape(1, 5 * D))
        in_maps.append(m)
    res = run_bass_kernel_spmd(nc, in_maps, core_ids=list(range(8)))
    out = np.stack([np.asarray(r["out"], dtype=np.float32) for r in res.results], axis=0)
    return out


# version 17
# speedup vs baseline: 1.0215x; 1.0215x over previous
import numpy as np
import concourse.bass as bass
import concourse.mybir as mybir
from concourse.bass_utils import run_bass_kernel_spmd
from contextlib import ExitStack

F32 = mybir.dt.float32
BF16 = mybir.dt.bfloat16
F16 = mybir.dt.float16
AF = mybir.ActivationFunctionType
ALU = mybir.AluOpType
AX = mybir.AxisListType
MAGIC = 12582912.0
EPS = 1e-5

T = 2048
D = 1024
NH = 16
NKC = 8
NT = 16
II = 2816
NI = 22


class Sem:
    def __init__(self, nc, es, name):
        self.h = es.enter_context(nc.semaphore(name))
        self.val = 0

    def tok(self):
        return (self, self.val)


class Prog:
    ENGS = ("sync", "gpsimd", "vector", "scalar", "tensor")

    def __init__(self, nc):
        self.nc = nc
        self.streams = {e: [] for e in self.ENGS}
        self.seen = {e: {} for e in self.ENGS}

    def w(self, eng, *toks):
        for t in toks:
            if t is None:
                continue
            s, v = t
            if v > 0 and self.seen[eng].get(id(s), -1) < v:
                self.seen[eng][id(s)] = v
                self.streams[eng].append(lambda e, s=s, v=v: e.wait_ge(s.h, v))

    def o(self, eng, name, *args, inc=None, incn=1, waits=(), **kw):
        self.w(eng, *waits)

        def f(e, name=name, args=args, kw=kw, inc=inc, incn=incn):
            ins = getattr(e, name)(*args, **kw)
            if inc is not None:
                ins.then_inc(inc.h, incn)

        self.streams[eng].append(f)
        if inc is not None:
            inc.val += incn
            return inc.tok()
        return None

    def dma(self, eng, out, in_, inc, waits=(), **kw):
        return self.o(eng, "dma_start", inc=inc, incn=16, waits=waits,
                      out=out, in_=in_, **kw)

    def run_block(self, name):
        with self.nc.Block(name) as block:
            for eng in self.ENGS:
                lst = self.streams[eng]
                if lst:
                    getattr(block, eng)(lambda e, lst=lst: [f(e) for f in lst])
        self.streams = {e: [] for e in self.ENGS}


def build_nc():
    nc = bass.Bass("TRN2", target_bir_lowering=False)

    x_in = nc.dram_tensor("x_in", [T, D], F32, kind="ExternalInput").ap()
    wq_d = nc.dram_tensor("wq", [D, D], F32, kind="ExternalInput").ap()
    wk_d = nc.dram_tensor("wk", [D, D], F32, kind="ExternalInput").ap()
    wv_d = nc.dram_tensor("wv", [D, D], F32, kind="ExternalInput").ap()
    wo_d = nc.dram_tensor("wo", [D, D], F32, kind="ExternalInput").ap()
    b_rows_d = nc.dram_tensor("b_rows", [1, 5 * D], F32, kind="ExternalInput").ap()
    wg_d = nc.dram_tensor("wg", [II, D], F32, kind="ExternalInput").ap()
    wu_d = nc.dram_tensor("wu", [II, D], F32, kind="ExternalInput").ap()
    wd_d = nc.dram_tensor("wd", [D, II], F32, kind="ExternalInput").ap()
    wm_d = nc.dram_tensor("wm", [D, 6 * D], F32, kind="ExternalInput").ap()
    clips_d = nc.dram_tensor("clips", [1, 2], F32, kind="ExternalInput").ap()
    out_d = nc.dram_tensor("out", [T, D], F32, kind="ExternalOutput").ap()

    v_dram = nc.dram_tensor("v_spill", [II, T], F16).ap()
    q_dram = nc.dram_tensor("q_spill", [128, 8, T], BF16).ap()
    h2_dram = nc.dram_tensor("h2_spill", [NT, 128, D], F32).ap()
    cc1_i = nc.dram_tensor("cc1_i", [1, 1], F32).ap()
    cc1_o = nc.dram_tensor("cc1_o", [1, 1], F32).ap()
    cc2_i = nc.dram_tensor("cc2_i", [1, 1], F32).ap()
    cc2_o = nc.dram_tensor("cc2_o", [1, 1], F32).ap()

    es0 = ExitStack()
    S = lambda n: Sem(nc, es0, n)
    d_x, d_a, d_b, d_wo, d_r = S("d_x"), S("d_a"), S("d_b"), S("d_wo"), S("d_r")
    d_q0, d_q1 = S("d_q0"), S("d_q1")
    d_wm0, d_wm1 = S("d_wm0"), S("d_wm1")
    d_f0, d_f1, d_wd = S("d_f0"), S("d_f1"), S("d_wd")
    d_sp, d_v0, d_v1, d_out, d_cc = S("d_sp"), S("d_v0"), S("d_v1"), S("d_out"), S("d_cc")
    d_h0, d_h1 = S("d_h0"), S("d_h1")
    d_wv0, d_wv1, d_u0, d_u1 = S("d_wv0"), S("d_wv1"), S("d_u0"), S("d_u1")
    pe, ve, se, ge, cc = S("pe"), S("ve"), S("se"), S("ge"), S("cc")

    sb = lambda st, n, shp, dt: st.enter_context(nc.sbuf_tensor(n, shp, dt))
    pt = lambda st, n, shp, dt: st.enter_context(nc.psum_tensor(n, shp, dt))

    x_tm = sb(es0, "x_tm", [128, NT, D], F32)
    ident_b = sb(es0, "ident_b", [128, 128], BF16)
    ident_f = sb(es0, "ident_f", [128, 128], F32)
    ones_f = sb(es0, "ones_f", [1, 128], F32)
    ga_bc = sb(es0, "ga_bc", [128, D], BF16)
    gabo_bc = sb(es0, "gabo_bc", [128, D], BF16)
    bv_bc = sb(es0, "bv_bc", [128, D], BF16)
    bqk_sb = sb(es0, "bqk_sb", [128, 16], F32)
    sa_f = sb(es0, "sa_f", [128, 8], F32)
    ca1_f = sb(es0, "ca1_f", [128, 8], F32)
    cb1_f = sb(es0, "cb1_f", [128, 8], F32)
    sb_f = sb(es0, "sb_f", [128, 8], F32)
    gb_f = sb(es0, "gb_f", [128, 8], F32)
    clip_sb = sb(es0, "clip_sb", [1, 2], F32)
    sc_sb = sb(es0, "sc_sb", [1, 4], F32)

    p = Prog(nc)

    def mk_stats(st, tag):
        return {
            "bn6": sb(st, "bn6" + tag, [128, 16, 12], F32),
            "mvb": sb(st, "mvb" + tag, [128, 16, 2], F32),
            "vrb": sb(st, "vrb" + tag, [128, 16], F32),
            "vinv": sb(st, "vinv" + tag, [128, 16], F32),
            "rstdb": sb(st, "rstdb" + tag, [128, 16], F32),
            "negmrb": sb(st, "negmrb" + tag, [128, 16], F32),
        }

    def ln_stats(sbuf, x_view, r0, ntl, waits=()):
        bn6, mvb = sbuf["bn6"], sbuf["mvb"]
        vrb, vinv = sbuf["vrb"], sbuf["vinv"]
        rstdb, negmrb = sbuf["rstdb"], sbuf["negmrb"]
        tk = None
        for tl in range(ntl):
            t = r0 + tl
            p.o("vector", "bn_stats", bn6[:, t, 0:6], x_view[:, t, 0:512],
                waits=waits if tl == 0 else ())
            tk = p.o("vector", "bn_stats", bn6[:, t, 6:12], x_view[:, t, 512:1024], inc=ve)
        p.w("vector", tk)
        for tl in range(ntl):
            t = r0 + tl
            tk = p.o("vector", "bn_aggr", mvb[:, t, :], bn6[:, t, :], inc=ve)
        p.w("vector", tk)
        tk = p.o("vector", "tensor_scalar", vrb[:, r0:r0 + ntl],
                 mvb[:, r0:r0 + ntl, 1], EPS, None, ALU.add, inc=ve)
        p.w("vector", tk)
        tk = p.o("vector", "reciprocal", vinv[:, r0:r0 + ntl], vrb[:, r0:r0 + ntl], inc=ve)
        t_sq = p.o("scalar", "activation", rstdb[:, r0:r0 + ntl], vinv[:, r0:r0 + ntl],
                   AF.Sqrt, waits=(tk,), inc=se)
        p.w("scalar", t_sq)
        t_nm = p.o("vector", "scalar_tensor_tensor", negmrb[:, r0:r0 + ntl],
                   mvb[:, r0:r0 + ntl, 0], -1.0, rstdb[:, r0:r0 + ntl],
                   ALU.mult, ALU.mult, waits=(t_sq,), inc=ve)
        return t_nm

    statsA = mk_stats(es0, "A")

    esS = ExitStack()
    wm_sb = sb(esS, "wm_sb", [128, 2, NKC, D], BF16)
    mod_sb = sb(esS, "mod_sb", [1, 6 * D], F32)
    rows = sb(esS, "rows", [1, 5 * D], F32)
    c_sb = sb(esS, "c_sb", [128, 8], F32)
    cs_sb = sb(esS, "cs_sb", [128, 8], BF16)
    gaf = sb(esS, "gaf", [1, 2, D], F32)
    rsc = sb(esS, "rsc", [1, 2, D], F32)
    ps_mod = pt(esS, "ps_mod", [1, 2, 512], F32)
    ps_bc = pt(esS, "ps_bc", [128, 2, 512], F32)
    ps_row = pt(esS, "ps_row", [128, 40], F32)

    x_r = x_in.rearrange("(n p) d -> p n d", p=128)
    t_xq = [p.dma("sync", x_tm[:, 4 * q:4 * q + 4, :], x_r[:, 4 * q:4 * q + 4, :],
                  (d_x, d_h0, d_h1, d_v0)[q]) for q in range(4)]
    t_rows0 = p.dma("sync", rows[:, :], b_rows_d, d_r)
    t_rows = p.dma("sync", clip_sb[:, :], clips_d, d_r, waits=(t_rows0,))

    wm_r = wm_d.rearrange("(c p) m -> p c m", p=128)
    t_wm = [None] * 6
    t_wm[0] = p.dma("gpsimd", wm_sb[:, 0, :, :], wm_r[:, :, 0:D], d_wm0)
    t_wm[1] = p.dma("gpsimd", wm_sb[:, 1, :, :], wm_r[:, :, D:2 * D], d_wm1)

    tgm = p.o("gpsimd", "memset", ident_b[:, :], 0.0, inc=ge)
    tgm2 = p.o("gpsimd", "memset", ident_f[:, :], 0.0, inc=ge)
    p.o("gpsimd", "affine_select", out=ident_b[:, :], in_=ident_b[:, :],
        compare_op=ALU.not_equal, fill=1.0, base=0,
        pattern=[[-1, 128]], channel_multiplier=1, waits=(tgm,))
    p.o("gpsimd", "affine_select", out=ident_f[:, :], in_=ident_f[:, :],
        compare_op=ALU.not_equal, fill=1.0, base=0,
        pattern=[[-1, 128]], channel_multiplier=1, waits=(tgm2,))
    p.o("gpsimd", "memset", ones_f[:, :], 1.0)
    t_ident = p.o("gpsimd", "memset", sc_sb[:, :], 0.0, inc=ge)

    t_rt = None
    for r in range(3):
        for ch in range(8):
            rr = r if r < 2 else 4
            t_rt = p.o("tensor", "transpose", ps_row[:, r * 8 + ch:r * 8 + ch + 1],
                       rows[:, rr * D + ch * 128: rr * D + (ch + 1) * 128],
                       ident_f[0:1, 0:1],
                       waits=(t_rows, t_ident) if (r == 0 and ch == 0) else (),
                       inc=pe if (r == 2 and ch == 7) else None)
    t_bqk = p.o("vector", "tensor_copy", bqk_sb[:, :], ps_row[:, 0:16], waits=(t_rt,), inc=ve)
    t_csb = p.o("vector", "tensor_copy", c_sb[:, :], ps_row[:, 16:24], inc=ve)
    t_cs = p.o("scalar", "activation", cs_sb[:, :], c_sb[:, :], AF.Silu,
               waits=(t_csb,), inc=se)

    t_mm = [None] * 12
    t_mev = [None] * 12
    for v in range(6):
        if v >= 2:
            sem_v = d_wm0 if v % 2 == 0 else d_wm1
            t_wm[v] = p.dma("gpsimd", wm_sb[:, v % 2, :, :],
                            wm_r[:, :, v * D:(v + 1) * D], sem_v,
                            waits=(t_mm[2 * (v - 2) + 1],))
        for jj in range(2):
            j = 2 * v + jj
            for kc_ in range(NKC):
                wl = [] if kc_ > 0 else [t_wm[v], t_cs] + ([t_mev[j - 2]] if j >= 2 else [])
                t_mm[j] = p.o("tensor", "matmul", ps_mod[:, j % 2, :],
                              cs_sb[:, kc_:kc_ + 1],
                              wm_sb[:, v % 2, kc_, jj * 512:jj * 512 + 512],
                              start=(kc_ == 0), stop=(kc_ == NKC - 1),
                              waits=tuple(wl), inc=pe if kc_ == NKC - 1 else None)
            t_mev[j] = p.o("vector", "tensor_copy", mod_sb[:, j * 512:(j + 1) * 512],
                           ps_mod[:, j % 2, :], waits=(t_mm[j],), inc=ve)

    t_gaf = [None, None]
    for gi, off in ((0, 2 * D), (1, 5 * D)):
        wl = (t_mev[off // 512], t_mev[off // 512 + 1])
        row = mod_sb[:, off:off + D]
        t_s1 = p.o("scalar", "activation", rsc[:, gi, :], row, AF.Sign, waits=wl, inc=se)
        t_s2 = p.o("scalar", "activation", gaf[:, gi, :], row, AF.Abs, inc=se)
        tmx_ = p.o("vector", "tensor_scalar", gaf[:, gi, :], gaf[:, gi, :], 0.25, None,
                   ALU.max, waits=(t_s1, t_s2), inc=ve)
        p.w("vector", tmx_)
        t_gaf[gi] = p.o("vector", "tensor_tensor", gaf[:, gi, :], gaf[:, gi, :],
                        rsc[:, gi, :], ALU.mult, inc=ve)

    t_rt2 = None
    vec_jobs = (("sa", mod_sb[:, 0:D], ()),
                ("ca", mod_sb[:, D:2 * D], ()),
                ("cb", mod_sb[:, 4 * D:5 * D], ()),
                ("sbv", mod_sb[:, 3 * D:4 * D], ()),
                ("gb", gaf[:, 1, :], (t_gaf[1],)))
    for vi, (nm, row, wl) in enumerate(vec_jobs):
        for ch in range(8):
            w0 = tuple(wl) + ((t_mev[11], t_bqk, t_csb) if (vi == 0 and ch == 0) else ())
            t_rt2 = p.o("tensor", "transpose", ps_row[:, vi * 8 + ch:vi * 8 + ch + 1],
                        row[:, ch * 128:(ch + 1) * 128],
                        ident_f[0:1, 0:1], waits=w0,
                        inc=pe if (vi == 4 and ch == 7) else None)
    p.o("vector", "tensor_copy", sa_f[:, :], ps_row[:, 0:8], waits=(t_rt2,))
    p.o("vector", "tensor_scalar", ca1_f[:, :], ps_row[:, 8:16], 1.0, None, ALU.add)
    p.o("vector", "tensor_scalar", cb1_f[:, :], ps_row[:, 16:24], 1.0, None, ALU.add)
    p.o("vector", "tensor_copy", sb_f[:, :], ps_row[:, 24:32])
    t_modf = p.o("vector", "tensor_copy", gb_f[:, :], ps_row[:, 32:40], inc=ve)

    bc_jobs = []
    bc_jobs.append(("ga", gaf[:, 0, :], ga_bc, None, (t_gaf[0],)))
    bc_jobs.append(("bv", rows[:, 2 * D:3 * D], bv_bc, None, (t_rows,)))
    bc_jobs.append(("bo", rows[:, 3 * D:4 * D], gabo_bc, "gabo", (t_rows,)))
    ps_free = [None, None]
    t_gabc = None
    for bi, (nm, row, outt, post, wl) in enumerate(bc_jobs):
        for half in range(2):
            slot = (2 * bi + half) % 2
            t_p = p.o("tensor", "matmul", ps_bc[:, slot, :], ones_f[:, :],
                      row[:, half * 512:half * 512 + 512], start=True, stop=True,
                      waits=tuple(wl) + ((ps_free[slot],) if ps_free[slot] else ()),
                      inc=pe)
            dst = outt[:, half * 512:half * 512 + 512]
            if post == "gabo":
                ps_free[slot] = p.o("vector", "tensor_tensor", dst, ps_bc[:, slot, :],
                                    ga_bc[:, half * 512:half * 512 + 512], ALU.mult,
                                    waits=(t_p, t_gabc), inc=ve)
            else:
                ps_free[slot] = p.o("vector", "tensor_copy", dst, ps_bc[:, slot, :],
                                    waits=(t_p,), inc=ve)
            if nm == "ga" and half == 1:
                t_gabc = ps_free[slot]
    t_setup_bc = ve.tok()

    t_nmA = [ln_stats(statsA, x_tm, 0, 8, waits=(t_xq[0], t_xq[1])),
             ln_stats(statsA, x_tm, 8, 8, waits=(t_xq[2], t_xq[3]))]
    p.run_block("setup")
    esS.close()

    esAB = ExitStack()
    q_cur = sb(esAB, "q_cur", [128, 2, NKC, 512], BF16)
    k_fm = sb(esAB, "k_fm", [128, NKC, T], BF16)
    v_tm = sb(esAB, "v_tm", [128, NT, NH, 65], BF16)

    esA = ExitStack()
    h1_half = sb(esA, "h1_half", [128, NKC, 1024], BF16)
    h1a = sb(esA, "h1a", [128, 2, D], F32)
    q_st = sb(esA, "q_st", [128, 2, 512], BF16)
    wqk_buf = sb(esA, "wqk_buf", [128, 2, NKC, 128], BF16)
    wv_buf = sb(esA, "wv_buf", [128, NKC, 512], BF16)
    ps_h1t = pt(esA, "ps_h1t", [128, 2, 1024], F32)
    ps_qkv = pt(esA, "ps_qkv", [128, 4, 512], F32)

    t_vones = p.o("gpsimd", "memset", v_tm[:, :, :, 64:65], 1.0, inc=ge)
    h1a_free = [None, None]
    qst_free = [None, None]
    psq_free = [None, None, None, None]
    psh_free = [None, None]
    wqk_free = [None, None]
    wq_r = wq_d.rearrange("(c p) m -> p c m", p=128)
    wk_r = wk_d.rearrange("(c p) m -> p c m", p=128)
    wv_r = wv_d.rearrange("(c p) m -> p c m", p=128)
    t_wv = None
    wv_free = [None, None]
    psq_i = 0

    for hf in range(2):
        r0 = hf * 8
        wq_r0 = wq_r if True else None
        pre_w = {}
        for mc in (0, 1):
            semw = d_a if mc == 0 else d_b
            pre_w[mc] = p.dma("gpsimd", wqk_buf[:, mc, :, :],
                              wq_r[:, :, mc * 128:(mc + 1) * 128], semw,
                              waits=(wqk_free[mc],))
            wqk_free[mc] = None
        t_nm = t_nmA[hf]
        t_h1ev = None
        for tl in range(8):
            t = r0 + tl
            par = tl % 2
            tA = p.o("scalar", "activation", h1a[:, par, :], x_tm[:, t, :], AF.Identity,
                     bias=statsA["negmrb"][:, t:t + 1], scale=statsA["rstdb"][:, t:t + 1],
                     waits=(t_nm, t_xq[2 * hf], t_xq[2 * hf + 1],
                            h1a_free[par]), inc=se)
            tD = None
            for ch in range(8):
                tD = p.o("tensor", "transpose", ps_h1t[:, par, ch * 128:(ch + 1) * 128],
                         h1a[:, par, ch * 128:(ch + 1) * 128], ident_f[:, :],
                         waits=(tA, psh_free[par]) if ch == 0 else (),
                         inc=pe if ch == 7 else None)
            h1a_free[par] = tD
            for ch in range(8):
                t_h1ev = p.o("vector", "tensor_scalar",
                             h1_half[:, ch, tl * 128:(tl + 1) * 128],
                             ps_h1t[:, par, ch * 128:(ch + 1) * 128],
                             ca1_f[:, ch:ch + 1], sa_f[:, ch:ch + 1],
                             ALU.mult, ALU.add,
                             waits=(tD, t_modf) if ch == 0 else (),
                             inc=ve if ch == 7 else None)
            psh_free[par] = t_h1ev

        for pi, (w_r, qk_out) in enumerate(((wq_r, None), (wk_r, k_fm))):
            for mc in range(8):
                slot = mc % 2
                semw = d_a if slot == 0 else d_b
                if pi == 0 and mc < 2:
                    t_w = pre_w[mc]
                else:
                    t_w = p.dma("gpsimd", wqk_buf[:, slot, :, :],
                                w_r[:, :, mc * 128:(mc + 1) * 128], semw,
                                waits=(wqk_free[slot],))
                wqk_free[slot] = None
                for g in range(2):
                    sl = psq_i % 4
                    psq_i += 1
                    tmm = None
                    for kc_ in range(NKC):
                        wl = (t_w, t_h1ev, psq_free[sl]) if kc_ == 0 else ()
                        tmm = p.o("tensor", "matmul", ps_qkv[:, sl, :],
                                  wqk_buf[:, slot, kc_, :],
                                  h1_half[:, kc_, g * 512:g * 512 + 512],
                                  start=(kc_ == 0), stop=(kc_ == NKC - 1),
                                  waits=wl, inc=pe if kc_ == NKC - 1 else None)
                    if pi == 0:
                        qsl = (mc * 2 + g) % 2
                        tev = p.o("vector", "tensor_scalar",
                                  q_st[:, qsl, :],
                                  ps_qkv[:, sl, :], bqk_sb[:, mc:mc + 1],
                                  None, ALU.add,
                                  waits=(tmm, t_bqk, qst_free[qsl]), inc=ve)
                        qst_free[qsl] = p.dma(
                            "sync", q_dram[:, mc, hf * 1024 + g * 512:
                                           hf * 1024 + g * 512 + 512],
                            q_st[:, qsl, :], d_h0 if qsl == 0 else d_h1,
                            waits=(tev,))
                    else:
                        tev = p.o("vector", "tensor_scalar",
                                  qk_out[:, mc, hf * 1024 + g * 512: hf * 1024 + g * 512 + 512],
                                  ps_qkv[:, sl, :], bqk_sb[:, pi * 8 + mc:pi * 8 + mc + 1],
                                  None, ALU.add, waits=(tmm, t_bqk), inc=ve)
                    psq_free[sl] = tev
                    if g == 1:
                        wqk_free[slot] = tev
        for vh in range(2):
            t_wv = p.dma("gpsimd", wv_buf[:, :, :], wv_r[:, :, vh * 512:vh * 512 + 512],
                         d_wv0 if vh == 0 else d_wv1,
                         waits=(wv_free[vh], wv_free[1 - vh]))
            for tl in range(8):
                t = r0 + tl
                sl = psq_i % 4
                psq_i += 1
                tmm = None
                for kc_ in range(NKC):
                    wl = (t_wv, t_h1ev, psq_free[sl]) if kc_ == 0 else ()
                    tmm = p.o("tensor", "matmul", ps_qkv[:, sl, :],
                              h1_half[:, kc_, tl * 128:(tl + 1) * 128],
                              wv_buf[:, kc_, :],
                              start=(kc_ == 0), stop=(kc_ == NKC - 1),
                              waits=wl, inc=pe if kc_ == NKC - 1 else None)
                tev = p.o("vector", "tensor_tensor",
                          v_tm[:, t, vh * 8:(vh + 1) * 8, 0:64],
                          ps_qkv[:, sl, :].rearrange("p (h e) -> p h e", h=8),
                          bv_bc[:, vh * 512:vh * 512 + 512].rearrange("p (h e) -> p h e", h=8),
                          ALU.add, waits=(tmm, t_setup_bc, t_vones), inc=ve)
                psq_free[sl] = tev
                if tl == 7:
                    wv_free[vh] = tev

    p.run_block("phaseA")
    esA.close()


    esB = ExitStack()
    wo_sb = sb(esB, "wo_sb", [128, NKC, D], BF16)
    P_sb = sb(esB, "P_sb", [128, 28, 512], BF16)
    stg_cfm = sb(esB, "stg_cfm", [128, 1024], BF16)
    stage = stg_cfm.ap()[0:65, :].rearrange("p (a b) -> p a b", a=2)
    ctx_fm = stg_cfm.ap().rearrange("p (c f) -> p c f", c=8)
    ctxt = sb(esB, "ctxt", [128, 4, NH, 64], BF16)
    rden = sb(esB, "rden", [128, 2, 4], F32)
    denm = sb(esB, "denm", [128, 2, 4], F32)
    ps_s = pt(esB, "ps_s", [128, 4, 512], F32)
    ps_av = pt(esB, "ps_av", [128, 2, 512], F32)
    ps_t0 = pt(esB, "ps_t0", [128, 1024], BF16)
    ps_t1 = pt(esB, "ps_t1", [128, 1024], BF16)
    ps_t = (ps_t0, ps_t1)

    t_wo = p.dma("gpsimd", wo_sb[:, :, :], wo_d.rearrange("(c p) m -> p c m", p=128), d_wo)
    t_wo2 = None
    for ch in range(NKC):
        t_wo2 = p.o("vector", "tensor_tensor", wo_sb[:, ch, :], wo_sb[:, ch, :],
                    ga_bc[:, :], ALU.mult,
                    waits=(t_wo, t_setup_bc) if ch == 0 else (),
                    inc=ve if ch == NKC - 1 else None)

    sbank_free = [None] * 4
    ppair_free = [None] * 14
    qcur_free = [None, None]
    t_qld = [None, None]
    av_free = [None, None]
    stage_free = [None, None]
    ta_free = [None, None]
    tb_free = [None, None]
    cfm_free = None
    po_free = None
    tb_last = None
    gsi = 0
    gp = 0
    norm_tok = [None] * NH
    pre_tok = [None] * 4

    pend_epi = None
    t_qld[0] = p.dma("sync", q_cur[:, 0, :, :], q_dram[:, :, 0:512], d_q0,
                     waits=(d_h0.tok(), d_h1.tok()))
    for qc in range(4):
        pre_tok[qc] = [None] * 4
        for j in range(4):
            pre_tok[qc][j] = p.o("gpsimd", "tensor_tensor", x_tm[:, qc * 4 + j, :],
                                 x_tm[:, qc * 4 + j, :], gabo_bc[:, :], ALU.add,
                                 waits=(t_setup_bc,) if (qc == 0 and j == 0) else (),
                                 inc=ge)

        exp_toks = [[None] * 8 for _ in range(NH)]
        exp_slot = [[0] * 8 for _ in range(NH)]
        av_tok = [None] * NH
        ta_tok = [None] * NH
        ev1_tok = [None] * NH

        def tb_task(j, qcb):
            nonlocal tb_last
            bank = ps_t[j % 2]
            ttb = None
            for ch in range(8):
                ttb = p.o("tensor", "transpose",
                          bank[:, ch * 128:(ch + 1) * 128],
                          ctxt[:, j, :, :].rearrange("p h e -> p (h e)")[:, ch * 128:(ch + 1) * 128],
                          ident_b[:, :],
                          waits=(last_norm, tb_free[j % 2]) if ch == 0 else (),
                          inc=pe if ch == 7 else None)
            tcf = p.o("vector", "tensor_copy", ctx_fm[:, :, :],
                      bank.ap().rearrange("p (c f) -> p c f", c=8),
                      waits=(ttb, cfm_free), inc=ve)
            tb_free[j % 2] = tcf
            ta_free[j % 2] = tcf
            if j == 3:
                tb_last = ttb
            return tcf

        def o_task(j, half, tcf, qcb):
            nonlocal cfm_free, po_free
            tom = None
            for kc_ in range(NKC):
                wl = (tcf, t_wo2, po_free, av_free[0]) if kc_ == 0 else ()
                tom = p.o("tensor", "matmul", ps_av[:, 0, :],
                          ctx_fm[:, kc_, :],
                          wo_sb[:, kc_, half * 512:half * 512 + 512],
                          start=(kc_ == 0), stop=(kc_ == NKC - 1),
                          waits=tuple(w for w in wl if w) if kc_ == 0 else (),
                          inc=pe if kc_ == NKC - 1 else None)
            if half == 1:
                cfm_free = tom
            po_free = p.o("vector", "tensor_tensor",
                          x_tm[:, qcb * 4 + j, half * 512:half * 512 + 512],
                          x_tm[:, qcb * 4 + j, half * 512:half * 512 + 512],
                          ps_av[:, 0, :], ALU.add,
                          waits=(tom, pre_tok[qcb][j]), inc=ve)
            av_free[0] = po_free

        def emit_exp(i, h, q_, tmm):
            nonlocal gp
            slot = gp % 14
            bank = 2 * (h % 2)
            texp = p.o("scalar", "activation",
                       P_sb[:, 2 * slot:2 * slot + 2, :].rearrange("p a b -> p (a b)"),
                       ps_s[:, bank:bank + 2, :].rearrange("p a b -> p (a b)"),
                       AF.Exp, scale=0.125,
                       waits=(tmm, ppair_free[slot]), inc=se)
            sbank_free[bank] = texp
            sbank_free[bank + 1] = texp
            exp_toks[h][q_] = texp
            exp_slot[h][q_] = slot
            gp += 1

        def s_half(i, half):
            h0, h1 = 2 * i, 2 * i + 1
            defer = []
            tmm = None
            for g in range(4 * half, 4 * half + 4):
                kc0 = 2 * g
                for h, kc_ in ((h0, kc0), (h0, kc0 + 1), (h1, kc0), (h1, kc0 + 1)):
                    bank = 2 * (h % 2) + kc_ % 2
                    base_p = (h % 2) * 64
                    wl = [sbank_free[bank]]
                    if i == 0 and g == 0:
                        wl.append(t_qld[qc % 2])
                    tmm = p.o("tensor", "matmul", ps_s[:, bank, :],
                              k_fm[base_p:base_p + 64, i, kc_ * 128:(kc_ + 1) * 128],
                              q_cur[base_p:base_p + 64, qc % 2, i, :],
                              start=True, stop=True,
                              waits=tuple(w for w in wl if w),
                              inc=pe if kc_ % 2 == 1 else None)
                    if kc_ % 2 == 1:
                        if g % 4 == 3:
                            defer.append((i, h, g, tmm))
                        else:
                            emit_exp(i, h, g, tmm)
            if i == NH // 2 - 1 and half == 1:
                qcur_free[qc % 2] = tmm
            return defer

        def av_half(i, half):
            h0, h1 = 2 * i, 2 * i + 1
            for q4 in range(2 * half, 2 * half + 2):
                for hp, h in ((0, h0), (1, h1)):
                    avl = None
                    for kav in range(4 * q4, 4 * q4 + 4):
                        q_ = kav // 2
                        slot = exp_slot[h][q_]
                        wl2 = [exp_toks[h][q_]]
                        if kav == 0:
                            wl2.append(av_free[hp])
                        avl = p.o("tensor", "matmul", ps_av[0:65, hp, :],
                                  v_tm[:, kav, h, :],
                                  P_sb[:, 2 * slot + kav % 2, :],
                                  start=(kav == 0), stop=(kav == 15),
                                  waits=tuple(w for w in wl2 if w),
                                  inc=pe if kav % 2 == 1 else None)
                        if kav % 2 == 1:
                            ppair_free[slot] = avl
                    if q4 == 3:
                        av_tok[h] = avl

        def ev1_pair(i):
            h0, h1 = 2 * i, 2 * i + 1
            for hp, h in ((0, h0), (1, h1)):
                ev1_tok[h] = p.o("vector", "tensor_copy", stage[:, hp, :],
                                 ps_av[0:65, hp, :],
                                 waits=(av_tok[h], stage_free[hp]), inc=ve)
                av_free[hp] = ev1_tok[h]

        def ta_pair(i):
            h0, h1 = 2 * i, 2 * i + 1
            for hp, h in ((0, h0), (1, h1)):
                tl = None
                for jj in range(4):
                    tl = p.o("tensor", "transpose",
                             ps_t[hp][:, jj * 66:jj * 66 + 65],
                             stage[0:65, hp, jj * 128:(jj + 1) * 128],
                             ident_b[0:65, 0:65],
                             waits=(ev1_tok[h], ta_free[hp]) if jj == 0 else (),
                             inc=pe if jj == 3 else None)
                ta_tok[h] = tl
                stage_free[hp] = tl

        def ev2_pair(i):
            h0, h1 = 2 * i, 2 * i + 1
            for hp, h in ((0, h0), (1, h1)):
                wl = [ta_tok[h]]
                if h == 0:
                    wl.append(tb_last)
                tav = ps_t[hp][:, 0:264].rearrange("p (a b) -> p a b", b=66)
                p.o("vector", "tensor_copy", ctxt[:, :, h, :], tav[:, :, 0:64],
                    waits=tuple(w for w in wl if w))
                tdc = p.o("vector", "tensor_copy", denm[:, hp, :],
                          tav[:, :, 64], inc=ve)
                ta_free[hp] = tdc
            p.w("vector", ta_free[1])
            t_rc = p.o("vector", "reciprocal", rden[:, :, :], denm[:, :, :], inc=ve)
            p.w("vector", t_rc)
            for hp, h in ((0, h0), (1, h1)):
                norm_tok[h] = None
                for jn in range(4):
                    norm_tok[h] = p.o("vector", "tensor_scalar", ctxt[:, jn, h, :],
                                      ctxt[:, jn, h, :], rden[:, hp, jn:jn + 1],
                                      None, ALU.mult, inc=ve if jn == 3 else None)

        NPAIR = NH // 2
        if qc + 1 < 4:
            t_qld[(qc + 1) % 2] = p.dma(
                "sync", q_cur[:, (qc + 1) % 2, :, :],
                q_dram[:, :, (qc + 1) * 512:(qc + 2) * 512],
                d_q0 if (qc + 1) % 2 == 0 else d_q1,
                waits=(qcur_free[(qc + 1) % 2],))
        for i in range(NPAIR):
            d3 = s_half(i, 0)
            if i > 0:
                av_half(i - 1, 1)
            elif pend_epi is not None:
                pend_epi()
            for args in d3:
                emit_exp(*args)
            if i > 0:
                ev1_pair(i - 1)
            d7 = s_half(i, 1)
            if i > 0:
                ta_pair(i - 1)
            av_half(i, 0)
            for args in d7:
                emit_exp(*args)
            if i > 0:
                ev2_pair(i - 1)
        av_half(NPAIR - 1, 1)
        ev1_pair(NPAIR - 1)
        ta_pair(NPAIR - 1)
        ev2_pair(NPAIR - 1)
        last_norm = norm_tok[NH - 1]

        def mk_epi(qcb, lnorm):
            def epi():
                nonlocal last_norm, pend_epi
                saved = last_norm
                last_norm = lnorm
                for jj in range(4):
                    tcf = tb_task(jj, qcb)
                    o_task(jj, 0, tcf, qcb)
                    o_task(jj, 1, tcf, qcb)
                stage_free[0] = cfm_free
                stage_free[1] = cfm_free
                last_norm = saved
                pend_epi = None
            return epi

        pend_epi = mk_epi(qc, last_norm)
        if qc == 3:
            pend_epi()

    p.run_block("phaseB")
    esB.close()
    esAB.close()

    esC = ExitStack()
    nwdT = sb(esC, "nwdT", [128, NI, D], BF16)
    G2_bc = sb(esC, "G2_bc", [128, D], F32)
    swd_all = sb(esC, "swd_all", [128, 8], F32)
    statsC = mk_stats(esC, "C")
    magic_col = sb(esC, "magic_col", [128, 1], F32)
    gmax1 = sb(esC, "gmax1", [1, 1], F32)
    gmax2 = sb(esC, "gmax2", [1, 1], F32)
    srow = sb(esC, "srow", [1, 4], F32)
    rsx1 = sb(esC, "rsx1", [128, 2], F32)
    rsx2 = sb(esC, "rsx2", [128, 2], F32)
    hred = sb(esC, "hred", [128, 24], F32)
    hacc = sb(esC, "hacc", [128, 1], F32)
    hrow = sb(esC, "hrow", [1, 128], F32)
    hl = sb(esC, "hl", [1, 1], F32)
    sig_sb = sb(esC, "sig_sb", [1, 2], F32)
    esPM = ExitStack()
    xn_fm = sb(esPM, "xn_fm", [128, NKC, T], BF16)
    ps_m = pt(esPM, "ps_m", [128, 512], F32)
    ps_r2 = pt(esPM, "ps_r2", [1, 1024], F32)
    esH = ExitStack()
    h2a = sb(esH, "h2a", [128, 2, D], F32)
    h2blk = sb(esH, "h2blk", [128, 2, D], F32)
    h2buf = sb(esH, "h2buf", [128, 2, 2 * D], F32)
    wd_nat = sb(esH, "wd_nat", [128, II], F32)
    nwd_nat = sb(esH, "nwd_nat", [128, II], BF16)
    mxw = sb(esH, "mxw", [128, 2, 2], F32)
    ps_h2 = pt(esH, "ps_h2", [128, 1024], F32)
    ps_tw = pt(esH, "ps_tw", [128, 1024], BF16)

    t_magic = p.o("gpsimd", "memset", magic_col[:, :], MAGIC, inc=ge)
    t_nm = ln_stats(statsC, x_tm, 0, 8)
    h2a_free = [None, None]
    h2blk_free = [None, None]
    psh2_free = None
    pstw_free = None
    t_swd = [None] * 8
    t_nwdT_done = None
    t_hm = None
    t_h2sp = [None] * NT
    for t in range(NT):
        if t == 8:
            t_nm = ln_stats(statsC, x_tm, 8, 8)
        par = t % 2
        tA = p.o("scalar", "activation", h2a[:, par, :], x_tm[:, t, :], AF.Identity,
                 bias=statsC["negmrb"][:, t:t + 1], scale=statsC["rstdb"][:, t:t + 1],
                 waits=(t_nm, h2a_free[par]), inc=se)
        tD = None
        for ch in range(8):
            tD = p.o("tensor", "transpose", ps_h2[:, ch * 128:(ch + 1) * 128],
                     h2a[:, par, ch * 128:(ch + 1) * 128], ident_f[:, :],
                     waits=(tA, psh2_free) if ch == 0 else (),
                     inc=pe if ch == 7 else None)
        h2a_free[par] = tD
        tev = None
        for ch in range(8):
            tev = p.o("vector", "tensor_scalar",
                      h2blk[:, par, ch * 128:(ch + 1) * 128],
                      ps_h2[:, ch * 128:(ch + 1) * 128],
                      cb1_f[:, ch:ch + 1], sb_f[:, ch:ch + 1],
                      ALU.mult, ALU.add,
                      waits=(tD, t_modf, h2blk_free[par]) if ch == 0 else (),
                      inc=ve if ch == 7 else None)
        psh2_free = tev
        p.w("vector", tev)
        t_hm = p.o("vector", "tensor_reduce", hred[:, t:t + 1], h2blk[:, par, :],
                   AX.X, ALU.max, apply_absolute_value=True, inc=ve)
        t_h2sp[t] = p.dma("sync", h2_dram[t, :, :], h2blk[:, par, :],
                          d_h0 if par == 0 else d_h1, waits=(t_hm,))
        h2blk_free[par] = t_h2sp[t]
    p.w("vector", t_hm)
    t_hm = p.o("vector", "tensor_reduce", hacc[:, :], hred[:, 0:NT], AX.X, ALU.max,
               apply_absolute_value=True, inc=ve)
    t_ht = p.o("tensor", "transpose", ps_r2[:, 0:128], hacc[:, :], ident_f[:, :],
               waits=(t_hm,), inc=pe)
    t_hr = p.o("vector", "tensor_copy", hrow[:, :], ps_r2[:, 0:128], waits=(t_ht,), inc=ve)
    p.w("vector", t_hr)
    t_hl = p.o("vector", "tensor_reduce", hl[:, :], hrow[:, :], AX.X, ALU.max, inc=ve)

    t_d1 = p.dma("gpsimd", cc1_i, hl[:, :], d_cc, waits=(t_hl,))
    t_cc1 = p.o("gpsimd", "collective_compute", "AllReduce", ALU.max,
                replica_groups=[list(range(8))], ins=[cc1_i], outs=[cc1_o],
                waits=(t_d1,), inc=cc)
    t_g1 = p.dma("gpsimd", gmax1[:, :], cc1_o, d_cc, waits=(t_cc1,))

    wd_state = {"free": None, "act": None, "nfree": None, "pstw": pstw_free,
                "done": None}

    def wd_chunk(hc):
        t_wdc = p.dma("sync", wd_nat[:, :], wd_d[hc * 128:(hc + 1) * 128, :],
                      d_wd, waits=(wd_state["free"],))
        tmx = p.o("vector", "tensor_reduce", mxw[:, 0, 0:1], wd_nat[:, :],
                  AX.X, ALU.max, apply_absolute_value=True, waits=(t_wdc,), inc=ve)
        p.w("vector", tmx)
        t_swd[hc] = p.o("vector", "tensor_scalar", swd_all[:, hc:hc + 1],
                        mxw[:, 0, 0:1], 1.0 / 127.0, None, ALU.mult, inc=ve)
        p.w("vector", t_swd[hc])
        trc = p.o("vector", "reciprocal", mxw[:, 0, 1:2], swd_all[:, hc:hc + 1],
                  waits=(wd_state["act"],), inc=ve)
        wd_state["act"] = p.o("scalar", "activation", wd_nat[:, :], wd_nat[:, :],
                              AF.Identity, scale=mxw[:, 0, 1:2],
                              bias=magic_col[:, 0:1],
                              waits=(trc, t_magic), inc=se)
        tqd = p.o("vector", "tensor_scalar", nwd_nat[:, :], wd_nat[:, :],
                  MAGIC, None, ALU.subtract,
                  waits=(wd_state["act"], wd_state["nfree"]), inc=ve)
        wd_state["free"] = tqd
        for b0 in range(0, NI, 8):
            nb = min(8, NI - b0)
            ttr = None
            for bi in range(nb):
                i = b0 + bi
                ttr = p.o("tensor", "transpose", ps_tw[:, bi * 128:(bi + 1) * 128],
                          nwd_nat[:, i * 128:(i + 1) * 128], ident_b[:, :],
                          waits=(tqd, wd_state["pstw"]) if bi == 0 else (),
                          inc=pe if bi == nb - 1 else None)
            wd_state["pstw"] = p.o("scalar", "activation",
                                   nwdT[:, b0:b0 + nb, hc * 128:(hc + 1) * 128],
                                   ps_tw.ap().rearrange("p (c f) -> p c f", c=8)[:, 0:nb, :],
                                   AF.Identity, waits=(ttr,), inc=se)
            wd_state["nfree"] = ttr
        wd_state["done"] = wd_state["pstw"]

    for hc in range(3):
        wd_chunk(hc)

    t_sg1 = p.o("scalar", "activation", sig_sb[:, 0:1], clip_sb[:, 0:1], AF.Sigmoid,
                waits=(t_rows,), inc=se)
    t_sg2 = p.o("scalar", "activation", sig_sb[:, 1:2], clip_sb[:, 1:2], AF.Sigmoid,
                inc=se)
    tq = p.o("vector", "tensor_tensor", sc_sb[:, 0:1], gmax1[:, :], sig_sb[:, 0:1],
             ALU.mult, waits=(t_g1, t_sg1), inc=ve)
    p.w("vector", tq)
    tq = p.o("vector", "reciprocal", sc_sb[:, 1:2], sc_sb[:, 0:1], inc=ve)
    p.w("vector", tq)
    p.o("vector", "tensor_scalar", srow[:, 0:1], sc_sb[:, 1:2], 127.0, None, ALU.mult)
    tq = p.o("vector", "tensor_scalar", srow[:, 1:2], sc_sb[:, 0:1], 1.0 / 127.0, None,
             ALU.mult, inc=ve)
    p.w("vector", tq)
    t_bq = p.o("tensor", "matmul", ps_m[:, 0:2], ones_f[:, :], srow[0:1, 0:2],
               start=True, stop=True, waits=(tq,), inc=pe)
    t_rsx1 = p.o("vector", "tensor_copy", rsx1[:, :], ps_m[:, 0:2], waits=(t_bq,), inc=ve)
    p.w("vector", t_rsx1)

    h2r = h2_dram.rearrange("t p (c f) -> p c t f", c=8)
    t_spall = (d_h0, d_h0.val)
    t_spall1 = (d_h1, d_h1.val)
    h2b_free = [None, None]
    t_xn_par = [None, None]
    t_ld = [None, None]
    t_ld[0] = p.dma("sync", h2buf[:, 0, :].rearrange("p (t f) -> p t f", t=NT),
                    h2r[:, 0, :, :], d_h0, waits=(t_spall, t_spall1))
    t_ld[1] = p.dma("sync", h2buf[:, 1, :].rearrange("p (t f) -> p t f", t=NT),
                    h2r[:, 1, :, :], d_h1)
    t_xnev = None
    for c0 in range(0, 8, 2):
        if c0 >= 2:
            t_ld[0] = p.dma("sync", h2buf[:, 0, :].rearrange("p (t f) -> p t f", t=NT),
                            h2r[:, c0, :, :], d_h0, waits=(h2b_free[0],))
            t_ld[1] = p.dma("sync", h2buf[:, 1, :].rearrange("p (t f) -> p t f", t=NT),
                            h2r[:, c0 + 1, :, :], d_h1, waits=(h2b_free[1],))
        for step in range(3):
            tstep = None
            for par in range(2):
                if step == 0:
                    tstep = p.o("vector", "tensor_scalar", h2buf[:, par, :],
                                h2buf[:, par, :], rsx1[:, 0:1], MAGIC, ALU.mult,
                                ALU.add, waits=(t_ld[par],), inc=ve if par else None)
                elif step == 1:
                    tstep = p.o("vector", "tensor_scalar", h2buf[:, par, :],
                                h2buf[:, par, :], MAGIC, 127.0, ALU.subtract, ALU.min,
                                inc=ve if par else None)
                else:
                    t_xnev = p.o("vector", "tensor_scalar", xn_fm[:, c0 + par, :],
                                 h2buf[:, par, :], -128.0, None, ALU.max, inc=ve)
                    h2b_free[par] = t_xnev
                    t_xn_par[par] = t_xnev
            if step < 2:
                p.w("vector", tstep)
        wd_chunk(3 + c0 // 2)
    wd_chunk(7)
    t_nwdT_done = wd_state["done"]
    esH.close()

    esL = ExitStack()
    rowC = sb(esL, "rowC", [1, D], F32)
    ps_t = pt(esL, "ps_t", [128, 1024], BF16)
    ps_g = pt(esL, "ps_g", [128, 2, 512], F32)
    ps_u = pt(esL, "ps_u", [128, 2, 512], F32)
    wg_nat = sb(esL, "wg_nat", [128, 2, D], F32)
    wu_nat = sb(esL, "wu_nat", [128, 2, D], F32)
    nw_nat = sb(esL, "nw_nat", [128, 2, 2, D], BF16)
    nwgT = sb(esL, "nwgT", [128, 2, NKC, 128], BF16)
    nwuT = sb(esL, "nwuT", [128, 2, NKC, 128], BF16)
    silu_sb = sb(esL, "silu_sb", [128, 2, 512], F32)
    v_sl = sb(esL, "v_sl", [128, T], F16)
    mxs = sb(esL, "mxs", [128, 4, 8], F32)

    wgn_free = [None, None]
    nwT_free = [[None, None], [None, None]]
    nwn_free = [[None, None], [None, None]]
    psg_free = [None, None]
    psu_free = [None, None]
    silu_free = [None, None]
    t_spill = None
    t_vmax = None
    t_sil_par = [None, None]
    t_v4_par = [None, None]
    pst_free = None
    t_f = [None, None]
    t_q = [[None, None], [None, None]]
    t_T = [[None, None], [None, None]]
    t_sv = [None, None]

    tg0 = p.dma("sync", wg_nat[:, 0, :], wg_d[0:128, :], d_f0,
                waits=(t_h2sp[NT - 2], t_h2sp[NT - 1], h2a_free[0], h2a_free[1]))
    tu0 = p.dma("sync", wu_nat[:, 0, :], wu_d[0:128, :], d_u0)
    t_f[0] = (tg0, tu0)
    tg1 = p.dma("sync", wg_nat[:, 1, :], wg_d[128:256, :], d_f1)
    tu1 = p.dma("sync", wu_nat[:, 1, :], wu_d[128:256, :], d_u1,
                waits=(t_xn_par[0],))
    t_f[1] = (tg1, tu1)

    def qd_stage(ic):
        par = ic % 2
        ql = ic % 4
        if ic >= 2:
            tgx = p.dma("sync", wg_nat[:, par, :], wg_d[ic * 128:(ic + 1) * 128, :],
                        d_f0 if par == 0 else d_f1, waits=(wgn_free[par],))
            tux = p.dma("sync", wu_nat[:, par, :], wu_d[ic * 128:(ic + 1) * 128, :],
                        d_u0 if par == 0 else d_u1)
            t_f[par] = (tgx, tux)
        wnats = (wg_nat, wu_nat)
        tl = None
        for wi in range(2):
            wl0 = (t_f[par] + (wd_state["nfree"],)) if (ic < 2 and wi == 0) else \
                  (t_f[par] if wi == 0 else ())
            tl = p.o("vector", "tensor_reduce", mxs[:, ql, wi:wi + 1],
                     wnats[wi][:, par, :], AX.X, ALU.max, apply_absolute_value=True,
                     waits=wl0, inc=ve if wi else None)
        p.w("vector", tl)
        for wi in range(2):
            tl = p.o("vector", "tensor_scalar", mxs[:, ql, wi:wi + 1],
                     mxs[:, ql, wi:wi + 1], 1.0 / 127.0, None, ALU.mult,
                     inc=ve if wi else None)
        p.w("vector", tl)
        for wi in range(2):
            tl = p.o("vector", "reciprocal", mxs[:, ql, 2 + wi:3 + wi],
                     mxs[:, ql, wi:wi + 1], inc=ve if wi else None)
        t_sv[par] = p.o("vector", "tensor_scalar", mxs[:, ql, 4:6],
                        mxs[:, ql, 0:2], rsx1[:, 1:2], None, ALU.mult,
                        waits=(t_sil_par[par], t_v4_par[par]), inc=ve)
        p.w("vector", t_sv[par])
        for wi in range(2):
            tl = p.o("vector", "tensor_scalar", wnats[wi][:, par, :],
                     wnats[wi][:, par, :], mxs[:, ql, 2 + wi:3 + wi], MAGIC,
                     ALU.mult, ALU.add, inc=ve if wi else None)
        p.w("vector", tl)
        for wi in range(2):
            t_q[par][wi] = p.o("vector", "tensor_scalar", nw_nat[:, par, wi, :],
                               wnats[wi][:, par, :], MAGIC, None, ALU.subtract,
                               waits=(nwn_free[par][wi],), inc=ve)
        wgn_free[par] = t_q[par][1]

    def qt_stage(ic):
        nonlocal pst_free
        par = ic % 2
        for wi, nwT in enumerate((nwgT, nwuT)):
            ttr = None
            for ch in range(8):
                ttr = p.o("tensor", "transpose", ps_t[:, ch * 128:(ch + 1) * 128],
                          nw_nat[:, par, wi, ch * 128:(ch + 1) * 128], ident_b[:, :],
                          waits=(t_q[par][wi], pst_free) if ch == 0 else (),
                          inc=pe if ch == 7 else None)
            nwn_free[par][wi] = ttr
            t_T[par][wi] = p.o("scalar", "activation", nwT[:, par, :, :],
                               ps_t.ap().rearrange("p (c f) -> p c f", c=8),
                               AF.Identity, waits=(ttr, nwT_free[par][wi]), inc=se)
            pst_free = t_T[par][wi]

    def mm_stage(ic):
        nonlocal t_spill, t_vmax
        par = ic % 2
        ql = ic % 4
        tg = tu = t_v4 = tsil = None
        for tc in range(4):
            gsl = tc % 2
            for kc_ in range(NKC):
                wl = (t_T[par][0], psg_free[gsl],
                      t_xnev if (ic == 0 and tc == 0) else None) if kc_ == 0 else ()
                tg = p.o("tensor", "matmul", ps_g[:, gsl, :],
                         nwgT[:, par, kc_, :], xn_fm[:, kc_, tc * 512:tc * 512 + 512],
                         start=(kc_ == 0), stop=(kc_ == NKC - 1),
                         waits=tuple(w for w in wl if w) if kc_ == 0 else (),
                         inc=pe if kc_ == NKC - 1 else None)
            tsil = p.o("scalar", "activation", silu_sb[:, gsl, :], ps_g[:, gsl, :],
                       AF.Silu, scale=mxs[:, ql, 4:5],
                       waits=(tg, t_sv[par], silu_free[gsl]), inc=se)
            psg_free[gsl] = tsil
            for kc_ in range(NKC):
                wl = (t_T[par][1], psu_free[gsl]) if kc_ == 0 else ()
                tu = p.o("tensor", "matmul", ps_u[:, gsl, :],
                         nwuT[:, par, kc_, :], xn_fm[:, kc_, tc * 512:tc * 512 + 512],
                         start=(kc_ == 0), stop=(kc_ == NKC - 1),
                         waits=wl, inc=pe if kc_ == NKC - 1 else None)
            t_v4 = p.o("vector", "scalar_tensor_tensor",
                       v_sl[:, tc * 512:tc * 512 + 512],
                       ps_u[:, gsl, :], mxs[:, ql, 5:6], silu_sb[:, gsl, :],
                       ALU.mult, ALU.mult,
                       waits=(tu, tsil, t_spill if tc == 0 else None,
                              t_vmax if tc == 0 else None), inc=ve)
            psu_free[gsl] = t_v4
            silu_free[gsl] = t_v4
        nwT_free[par] = [tg, tu]
        t_sil_par[par] = tsil
        t_v4_par[par] = t_v4
        t_vmax = p.o("vector", "tensor_reduce", hred[:, ic:ic + 1], v_sl[:, :],
                     AX.X, ALU.max, apply_absolute_value=True, waits=(t_v4,), inc=ve)
        t_spill = p.dma("sync", v_dram[ic * 128:(ic + 1) * 128, :], v_sl[:, :], d_sp,
                        waits=(t_v4,))

    qd_stage(0)
    qd_stage(1)
    qt_stage(0)
    for ic in range(NI):
        if ic + 2 < NI:
            qd_stage(ic + 2)
        mm_stage(ic)
        if ic + 1 < NI:
            qt_stage(ic + 1)

    p.w("vector", t_vmax)
    t_hm2 = p.o("vector", "tensor_reduce", hacc[:, :], hred[:, 0:NI], AX.X, ALU.max,
                apply_absolute_value=True, inc=ve)
    t_ht2 = p.o("tensor", "transpose", ps_r2[:, 0:128], hacc[:, :], ident_f[:, :],
                waits=(t_hm2,), inc=pe)
    t_hr2 = p.o("vector", "tensor_copy", hrow[:, :], ps_r2[:, 0:128], waits=(t_ht2,), inc=ve)
    p.w("vector", t_hr2)
    t_hl2 = p.o("vector", "tensor_reduce", hl[:, :], hrow[:, :], AX.X, ALU.max, inc=ve)
    t_d2 = p.dma("gpsimd", cc2_i, hl[:, :], d_cc, waits=(t_hl2,))
    t_cc2 = p.o("gpsimd", "collective_compute", "AllReduce", ALU.max,
                replica_groups=[list(range(8))], ins=[cc2_i], outs=[cc2_o],
                waits=(t_d2,), inc=cc)
    t_g2 = p.dma("gpsimd", gmax2[:, :], cc2_o, d_cc, waits=(t_cc2,))
    tq = p.o("vector", "tensor_tensor", sc_sb[:, 2:3], gmax2[:, :], sig_sb[:, 1:2],
             ALU.mult, waits=(t_g2, t_sg2), inc=ve)
    p.w("vector", tq)
    tq = p.o("vector", "reciprocal", sc_sb[:, 3:4], sc_sb[:, 2:3], inc=ve)
    p.w("vector", tq)
    p.o("vector", "tensor_scalar", srow[:, 2:3], sc_sb[:, 3:4], 127.0, None, ALU.mult)
    tq = p.o("vector", "tensor_scalar", srow[:, 3:4], sc_sb[:, 2:3], 1.0 / 127.0,
             None, ALU.mult, inc=ve)
    p.w("vector", tq)
    t_bq2 = p.o("tensor", "matmul", ps_m[:, 0:2], ones_f[:, :], srow[0:1, 2:4],
                start=True, stop=True, waits=(tq,), inc=pe)
    t_rsx2 = p.o("vector", "tensor_copy", rsx2[:, :], ps_m[:, 0:2], waits=(t_bq2,), inc=ve)
    p.w("vector", t_rsx2)
    t_swdg = p.o("vector", "tensor_tensor", swd_all[:, :], swd_all[:, :],
                 gb_f[:, :], ALU.mult, waits=(t_swd[7],), inc=ve)
    p.w("vector", t_swdg)
    tsr = None
    for hc in range(8):
        tsr = p.o("tensor", "transpose", ps_r2[:, hc * 128:(hc + 1) * 128],
                  swd_all[:, hc:hc + 1], ident_f[:, :],
                  waits=(t_swdg,) if hc == 0 else (),
                  inc=pe if hc == 7 else None)
    t_row = p.o("vector", "tensor_copy", rowC[:, :], ps_r2[:, 0:1024], waits=(tsr,), inc=ve)
    p.w("vector", t_row)
    t_G2 = None
    for half in range(2):
        tbb = p.o("tensor", "matmul", ps_m[:, :], ones_f[:, :],
                  rowC[0:1, half * 512:half * 512 + 512], start=True, stop=True,
                  waits=(t_row,) if half == 0 else (t_G2,), inc=pe)
        t_G2 = p.o("vector", "tensor_scalar", G2_bc[:, half * 512:half * 512 + 512],
                   ps_m[:, :], rsx2[:, 1:2], None,
                   ALU.mult, waits=(tbb,), inc=ve)

    p.run_block("phaseC")
    esL.close()
    esPM.close()

    esD = ExitStack()
    v_tc = sb(esD, "v_tc", [128, 2, NI, 512], F16)
    n2 = sb(esD, "n2", [128, 2, 512], BF16)
    tq1 = sb(esD, "tq1", [128, 2, 512], F32)
    tmpe = sb(esD, "tmpe", [128, 4, 512], F32)
    ps_d = pt(esD, "ps_d", [128, 8, 512], F32)

    v_r = v_dram.rearrange("(c p) t -> p c t", p=128)
    t_vld = [None] * 4
    vtc_free = [None, None]
    t_spill_all = d_sp.tok()

    def v_load(tc, waits=()):
        s0, s1 = (d_v0, d_v1) if tc % 2 == 0 else (d_f0, d_f1)
        ta_ = p.dma("sync", v_tc[:, tc % 2, 0:11, :],
                    v_r[:, 0:11, tc * 512:tc * 512 + 512], s0, waits=waits)
        tb_ = p.dma("sync", v_tc[:, tc % 2, 11:NI, :],
                    v_r[:, 11:NI, tc * 512:tc * 512 + 512], s1, waits=waits)
        return (ta_, tb_)

    t_vld[0] = v_load(0, waits=(t_spill_all,))
    t_vld[1] = v_load(1)

    n2_free = [None, None]
    tq_free = [None, None]
    tmpe_free = [None] * 4
    t_out = None
    for tc in range(4):
        par = tc % 2
        mm_tok = [None] * 8
        t_icmm = None
        t_n2p = [None, None]
        for ic0 in range(0, NI, 2):
            t_s0 = [None, None]
            for npar in range(2):
                ic = ic0 + npar
                t_s0[npar] = p.o("scalar", "activation", tq1[:, npar, :],
                                 v_tc[:, par, ic, :], AF.Identity,
                                 scale=rsx2[:, 0:1], bias=magic_col[:, 0:1],
                                 waits=t_vld[tc] + (t_rsx2, t_magic, tq_free[npar])
                                 if ic < 2 else (tq_free[npar],), inc=se)
            for step in range(2):
                tstep = None
                for npar in range(2):
                    if step == 0:
                        tstep = p.o("vector", "tensor_scalar", tq1[:, npar, :],
                                    tq1[:, npar, :], MAGIC, 127.0,
                                    ALU.subtract, ALU.min,
                                    waits=(t_s0[npar],), inc=ve if npar else None)
                    else:
                        t_n2p[npar] = p.o("vector", "tensor_scalar", n2[:, npar, :],
                                          tq1[:, npar, :], -128.0, None, ALU.max,
                                          waits=(n2_free[npar],), inc=ve)
                        tq_free[npar] = t_n2p[npar]
                if step < 1:
                    p.w("vector", tstep)
            for npar in range(2):
                ic = ic0 + npar
                t_n2 = t_n2p[npar]
                for tt in range(4):
                    for hh in range(2):
                        k = tt * 2 + hh
                        wl = []
                        if ic == 0:
                            wl = [t_n2, t_nwdT_done]
                            if tc >= 1:
                                wl.append(mm_free[k])
                        elif tt == 0 and hh == 0:
                            wl = [t_n2]
                        t_icmm = p.o("tensor", "matmul", ps_d[:, k, :],
                                     n2[:, npar, tt * 128:(tt + 1) * 128],
                                     nwdT[:, ic, hh * 512:hh * 512 + 512],
                                     start=(ic == 0), stop=(ic == NI - 1),
                                     waits=tuple(wl),
                                     inc=pe if (ic == NI - 1 or (tt == 3 and hh == 1)) else None)
                        if ic == NI - 1:
                            mm_tok[k] = t_icmm
                n2_free[npar] = t_icmm
                if ic == NI - 1:
                    vtc_free[par] = t_icmm
        if tc + 2 < 4:
            t_vld[tc + 2] = v_load(tc + 2, waits=(vtc_free[par],))
        mm_free = [None] * 8
        txa_tok = [None] * 8
        for grp in range(2):
            tmu_last = None
            for kk in range(4):
                k = grp * 4 + kk
                tt, hh = k // 2, k % 2
                tmu_last = p.o("vector", "tensor_tensor", tmpe[:, kk, :], ps_d[:, k, :],
                               G2_bc[:, hh * 512:hh * 512 + 512], ALU.mult,
                               waits=(mm_tok[k], t_G2, tmpe_free[kk]), inc=ve)
                mm_free[k] = tmu_last
            p.w("vector", tmu_last)
            txa_last = None
            for kk in range(4):
                k = grp * 4 + kk
                tt, hh = k // 2, k % 2
                txa_last = p.o("vector", "tensor_tensor",
                               x_tm[:, tc * 4 + tt, hh * 512:hh * 512 + 512],
                               x_tm[:, tc * 4 + tt, hh * 512:hh * 512 + 512],
                               tmpe[:, kk, :], ALU.add, inc=ve)
                tmpe_free[kk] = txa_last
                txa_tok[k] = txa_last
            p.w("vector", txa_last)
        for tt in range(4):
            is_last = (tc == 3 and tt == 3)
            t_out = p.dma("sync",
                          out_d.rearrange("(n p) d -> p n d", p=128)[:, tc * 4 + tt, :],
                          x_tm[:, tc * 4 + tt, :], d_out,
                          waits=(txa_tok[tt * 2 + 1],) + (((d_out, 240),) if is_last else ()))
    p.w("sync", d_out.tok())
    p.run_block("phaseD")
    esD.close()
    esC.close()
    es0.close()
    return nc


_NC_CACHE = None


def kernel(**inputs):
    global _NC_CACHE
    f32 = lambda k: np.ascontiguousarray(np.asarray(inputs[k], dtype=np.float32))
    x = f32("L0_attn_x")
    c = f32("L0_c")
    b_rows_base = np.stack([
        f32("L0_attn_bq").reshape(D), f32("L0_attn_bk").reshape(D),
        f32("L0_attn_bv").reshape(D), f32("L0_attn_bo").reshape(D),
        np.zeros(D, np.float32)], axis=0)
    clips = np.stack([f32("L0_ffn_clip_x").reshape(()),
                      f32("L0_ffn_clip_down").reshape(())]).reshape(1, 2)
    common = {
        "wq": f32("L0_attn_Wq"), "wk": f32("L0_attn_Wk"),
        "wv": f32("L0_attn_Wv"), "wo": f32("L0_attn_Wo"),
        "wg": f32("L0_ffn_w_gate"), "wu": f32("L0_ffn_w_up"),
        "wd": f32("L0_ffn_w_down"), "wm": f32("L0_wmod"),
        "clips": np.ascontiguousarray(clips),
    }
    if _NC_CACHE is None:
        _NC_CACHE = build_nc()
    nc = _NC_CACHE
    in_maps = []
    for b in range(8):
        rows_b = b_rows_base.copy()
        rows_b[4] = c[b]
        m = dict(common)
        m["x_in"] = np.ascontiguousarray(x[b])
        m["b_rows"] = np.ascontiguousarray(rows_b.reshape(1, 5 * D))
        in_maps.append(m)
    res = run_bass_kernel_spmd(nc, in_maps, core_ids=list(range(8)))
    out = np.stack([np.asarray(r["out"], dtype=np.float32) for r in res.results], axis=0)
    return out


# version 18
# speedup vs baseline: 1.0431x; 1.0212x over previous
import numpy as np
import concourse.bass as bass
import concourse.mybir as mybir
from concourse.bass_utils import run_bass_kernel_spmd
from contextlib import ExitStack

F32 = mybir.dt.float32
BF16 = mybir.dt.bfloat16
F16 = mybir.dt.float16
AF = mybir.ActivationFunctionType
ALU = mybir.AluOpType
AX = mybir.AxisListType
MAGIC = 12582912.0
EPS = 1e-5

T = 2048
D = 1024
NH = 16
NKC = 8
NT = 16
II = 2816
NI = 22


class Sem:
    def __init__(self, nc, es, name):
        self.h = es.enter_context(nc.semaphore(name))
        self.val = 0

    def tok(self):
        return (self, self.val)


class Prog:
    ENGS = ("sync", "gpsimd", "vector", "scalar", "tensor")

    def __init__(self, nc):
        self.nc = nc
        self.streams = {e: [] for e in self.ENGS}
        self.seen = {e: {} for e in self.ENGS}

    def w(self, eng, *toks):
        for t in toks:
            if t is None:
                continue
            s, v = t
            if v > 0 and self.seen[eng].get(id(s), -1) < v:
                self.seen[eng][id(s)] = v
                self.streams[eng].append(lambda e, s=s, v=v: e.wait_ge(s.h, v))

    def o(self, eng, name, *args, inc=None, incn=1, waits=(), **kw):
        self.w(eng, *waits)

        def f(e, name=name, args=args, kw=kw, inc=inc, incn=incn):
            ins = getattr(e, name)(*args, **kw)
            if inc is not None:
                ins.then_inc(inc.h, incn)

        self.streams[eng].append(f)
        if inc is not None:
            inc.val += incn
            return inc.tok()
        return None

    def dma(self, eng, out, in_, inc, waits=(), **kw):
        return self.o(eng, "dma_start", inc=inc, incn=16, waits=waits,
                      out=out, in_=in_, **kw)

    def run_block(self, name):
        with self.nc.Block(name) as block:
            for eng in self.ENGS:
                lst = self.streams[eng]
                if lst:
                    getattr(block, eng)(lambda e, lst=lst: [f(e) for f in lst])
        self.streams = {e: [] for e in self.ENGS}


def build_nc():
    nc = bass.Bass("TRN2", target_bir_lowering=False)

    x_in = nc.dram_tensor("x_in", [T, D], F32, kind="ExternalInput").ap()
    wq_d = nc.dram_tensor("wq", [D, D], F32, kind="ExternalInput").ap()
    wk_d = nc.dram_tensor("wk", [D, D], F32, kind="ExternalInput").ap()
    wv_d = nc.dram_tensor("wv", [D, D], F32, kind="ExternalInput").ap()
    wo_d = nc.dram_tensor("wo", [D, D], F32, kind="ExternalInput").ap()
    b_rows_d = nc.dram_tensor("b_rows", [1, 5 * D], F32, kind="ExternalInput").ap()
    wg_d = nc.dram_tensor("wg", [II, D], F32, kind="ExternalInput").ap()
    wu_d = nc.dram_tensor("wu", [II, D], F32, kind="ExternalInput").ap()
    wd_d = nc.dram_tensor("wd", [D, II], F32, kind="ExternalInput").ap()
    wm_d = nc.dram_tensor("wm", [D, 6 * D], F32, kind="ExternalInput").ap()
    clips_d = nc.dram_tensor("clips", [1, 2], F32, kind="ExternalInput").ap()
    out_d = nc.dram_tensor("out", [T, D], F32, kind="ExternalOutput").ap()

    v_dram = nc.dram_tensor("v_spill", [II, T], F16).ap()
    q_dram = nc.dram_tensor("q_spill", [128, 8, T], BF16).ap()
    h2_dram = nc.dram_tensor("h2_spill", [NT, 128, D], F32).ap()
    cc1_i = nc.dram_tensor("cc1_i", [1, 1], F32).ap()
    cc1_o = nc.dram_tensor("cc1_o", [1, 1], F32).ap()
    cc2_i = nc.dram_tensor("cc2_i", [1, 1], F32).ap()
    cc2_o = nc.dram_tensor("cc2_o", [1, 1], F32).ap()

    es0 = ExitStack()
    S = lambda n: Sem(nc, es0, n)
    d_x, d_a, d_b, d_wo, d_r = S("d_x"), S("d_a"), S("d_b"), S("d_wo"), S("d_r")
    d_q0, d_q1 = S("d_q0"), S("d_q1")
    d_wm0, d_wm1 = S("d_wm0"), S("d_wm1")
    d_f0, d_f1, d_wd = S("d_f0"), S("d_f1"), S("d_wd")
    d_sp, d_v0, d_v1, d_out, d_cc = S("d_sp"), S("d_v0"), S("d_v1"), S("d_out"), S("d_cc")
    d_h0, d_h1 = S("d_h0"), S("d_h1")
    d_wv0, d_wv1, d_u0, d_u1 = S("d_wv0"), S("d_wv1"), S("d_u0"), S("d_u1")
    pe, ve, se, ge, cc = S("pe"), S("ve"), S("se"), S("ge"), S("cc")

    sb = lambda st, n, shp, dt: st.enter_context(nc.sbuf_tensor(n, shp, dt))
    pt = lambda st, n, shp, dt: st.enter_context(nc.psum_tensor(n, shp, dt))

    x_tm = sb(es0, "x_tm", [128, NT, D], F32)
    ident_b = sb(es0, "ident_b", [128, 128], BF16)
    ident_f = sb(es0, "ident_f", [128, 128], F32)
    ones_f = sb(es0, "ones_f", [1, 128], F32)
    ga_bc = sb(es0, "ga_bc", [128, D], BF16)
    gabo_bc = sb(es0, "gabo_bc", [128, D], BF16)
    bv_bc = sb(es0, "bv_bc", [128, D], BF16)
    bqk_sb = sb(es0, "bqk_sb", [128, 16], F32)
    sa_f = sb(es0, "sa_f", [128, 8], F32)
    ca1_f = sb(es0, "ca1_f", [128, 8], F32)
    cb1_f = sb(es0, "cb1_f", [128, 8], F32)
    sb_f = sb(es0, "sb_f", [128, 8], F32)
    gb_f = sb(es0, "gb_f", [128, 8], F32)
    clip_sb = sb(es0, "clip_sb", [1, 2], F32)
    sc_sb = sb(es0, "sc_sb", [1, 4], F32)

    p = Prog(nc)

    def mk_stats(st, tag):
        return {
            "bn6": sb(st, "bn6" + tag, [128, 16, 12], F32),
            "mvb": sb(st, "mvb" + tag, [128, 16, 2], F32),
            "vrb": sb(st, "vrb" + tag, [128, 16], F32),
            "vinv": sb(st, "vinv" + tag, [128, 16], F32),
            "rstdb": sb(st, "rstdb" + tag, [128, 16], F32),
            "negmrb": sb(st, "negmrb" + tag, [128, 16], F32),
        }

    def ln_stats(sbuf, x_view, r0, ntl, waits=()):
        bn6, mvb = sbuf["bn6"], sbuf["mvb"]
        vrb, vinv = sbuf["vrb"], sbuf["vinv"]
        rstdb, negmrb = sbuf["rstdb"], sbuf["negmrb"]
        tk = None
        for tl in range(ntl):
            t = r0 + tl
            p.o("vector", "bn_stats", bn6[:, t, 0:6], x_view[:, t, 0:512],
                waits=waits if tl == 0 else ())
            tk = p.o("vector", "bn_stats", bn6[:, t, 6:12], x_view[:, t, 512:1024], inc=ve)
        p.w("vector", tk)
        for tl in range(ntl):
            t = r0 + tl
            tk = p.o("vector", "bn_aggr", mvb[:, t, :], bn6[:, t, :], inc=ve)
        p.w("vector", tk)
        tk = p.o("vector", "tensor_scalar", vrb[:, r0:r0 + ntl],
                 mvb[:, r0:r0 + ntl, 1], EPS, None, ALU.add, inc=ve)
        p.w("vector", tk)
        tk = p.o("vector", "reciprocal", vinv[:, r0:r0 + ntl], vrb[:, r0:r0 + ntl], inc=ve)
        t_sq = p.o("scalar", "activation", rstdb[:, r0:r0 + ntl], vinv[:, r0:r0 + ntl],
                   AF.Sqrt, waits=(tk,), inc=se)
        p.w("scalar", t_sq)
        t_nm = p.o("vector", "scalar_tensor_tensor", negmrb[:, r0:r0 + ntl],
                   mvb[:, r0:r0 + ntl, 0], -1.0, rstdb[:, r0:r0 + ntl],
                   ALU.mult, ALU.mult, waits=(t_sq,), inc=ve)
        return t_nm

    statsA = mk_stats(es0, "A")

    esS = ExitStack()
    wm_sb = sb(esS, "wm_sb", [128, 2, NKC, D], BF16)
    mod_sb = sb(esS, "mod_sb", [1, 6 * D], F32)
    rows = sb(esS, "rows", [1, 5 * D], F32)
    c_sb = sb(esS, "c_sb", [128, 8], F32)
    cs_sb = sb(esS, "cs_sb", [128, 8], BF16)
    gaf = sb(esS, "gaf", [1, 2, D], F32)
    rsc = sb(esS, "rsc", [1, 2, D], F32)
    ps_mod = pt(esS, "ps_mod", [1, 2, 512], F32)
    ps_bc = pt(esS, "ps_bc", [128, 2, 512], F32)
    ps_row = pt(esS, "ps_row", [128, 40], F32)

    x_r = x_in.rearrange("(n p) d -> p n d", p=128)
    t_xq = [p.dma("sync", x_tm[:, 4 * q:4 * q + 4, :], x_r[:, 4 * q:4 * q + 4, :],
                  (d_x, d_h0, d_h1, d_v0)[q]) for q in range(4)]
    t_rows0 = p.dma("sync", rows[:, :], b_rows_d, d_r)
    t_rows = p.dma("sync", clip_sb[:, :], clips_d, d_r, waits=(t_rows0,))

    wm_r = wm_d.rearrange("(c p) m -> p c m", p=128)
    t_wm = [None] * 6
    t_wm[0] = p.dma("gpsimd", wm_sb[:, 0, :, :], wm_r[:, :, 0:D], d_wm0)
    t_wm[1] = p.dma("gpsimd", wm_sb[:, 1, :, :], wm_r[:, :, D:2 * D], d_wm1)

    tgm = p.o("gpsimd", "memset", ident_b[:, :], 0.0, inc=ge)
    tgm2 = p.o("gpsimd", "memset", ident_f[:, :], 0.0, inc=ge)
    p.o("gpsimd", "affine_select", out=ident_b[:, :], in_=ident_b[:, :],
        compare_op=ALU.not_equal, fill=1.0, base=0,
        pattern=[[-1, 128]], channel_multiplier=1, waits=(tgm,))
    p.o("gpsimd", "affine_select", out=ident_f[:, :], in_=ident_f[:, :],
        compare_op=ALU.not_equal, fill=1.0, base=0,
        pattern=[[-1, 128]], channel_multiplier=1, waits=(tgm2,))
    p.o("gpsimd", "memset", ones_f[:, :], 1.0)
    t_ident = p.o("gpsimd", "memset", sc_sb[:, :], 0.0, inc=ge)

    t_rt = None
    for r in range(3):
        for ch in range(8):
            rr = r if r < 2 else 4
            t_rt = p.o("tensor", "transpose", ps_row[:, r * 8 + ch:r * 8 + ch + 1],
                       rows[:, rr * D + ch * 128: rr * D + (ch + 1) * 128],
                       ident_f[0:1, 0:1],
                       waits=(t_rows, t_ident) if (r == 0 and ch == 0) else (),
                       inc=pe if (r == 2 and ch == 7) else None)
    t_bqk = p.o("vector", "tensor_copy", bqk_sb[:, :], ps_row[:, 0:16], waits=(t_rt,), inc=ve)
    t_csb = p.o("vector", "tensor_copy", c_sb[:, :], ps_row[:, 16:24], inc=ve)
    t_cs = p.o("scalar", "activation", cs_sb[:, :], c_sb[:, :], AF.Silu,
               waits=(t_csb,), inc=se)

    t_mm = [None] * 12
    t_mev = [None] * 12
    for v in range(6):
        if v >= 2:
            sem_v = d_wm0 if v % 2 == 0 else d_wm1
            t_wm[v] = p.dma("gpsimd", wm_sb[:, v % 2, :, :],
                            wm_r[:, :, v * D:(v + 1) * D], sem_v,
                            waits=(t_mm[2 * (v - 2) + 1],))
        for jj in range(2):
            j = 2 * v + jj
            for kc_ in range(NKC):
                wl = [] if kc_ > 0 else [t_wm[v], t_cs] + ([t_mev[j - 2]] if j >= 2 else [])
                t_mm[j] = p.o("tensor", "matmul", ps_mod[:, j % 2, :],
                              cs_sb[:, kc_:kc_ + 1],
                              wm_sb[:, v % 2, kc_, jj * 512:jj * 512 + 512],
                              start=(kc_ == 0), stop=(kc_ == NKC - 1),
                              waits=tuple(wl), inc=pe if kc_ == NKC - 1 else None)
            t_mev[j] = p.o("vector", "tensor_copy", mod_sb[:, j * 512:(j + 1) * 512],
                           ps_mod[:, j % 2, :], waits=(t_mm[j],), inc=ve)

    t_gaf = [None, None]
    for gi, off in ((0, 2 * D), (1, 5 * D)):
        wl = (t_mev[off // 512], t_mev[off // 512 + 1])
        row = mod_sb[:, off:off + D]
        t_s1 = p.o("scalar", "activation", rsc[:, gi, :], row, AF.Sign, waits=wl, inc=se)
        t_s2 = p.o("scalar", "activation", gaf[:, gi, :], row, AF.Abs, inc=se)
        tmx_ = p.o("vector", "tensor_scalar", gaf[:, gi, :], gaf[:, gi, :], 0.25, None,
                   ALU.max, waits=(t_s1, t_s2), inc=ve)
        p.w("vector", tmx_)
        t_gaf[gi] = p.o("vector", "tensor_tensor", gaf[:, gi, :], gaf[:, gi, :],
                        rsc[:, gi, :], ALU.mult, inc=ve)

    t_rt2 = None
    vec_jobs = (("sa", mod_sb[:, 0:D], ()),
                ("ca", mod_sb[:, D:2 * D], ()),
                ("cb", mod_sb[:, 4 * D:5 * D], ()),
                ("sbv", mod_sb[:, 3 * D:4 * D], ()),
                ("gb", gaf[:, 1, :], (t_gaf[1],)))
    for vi, (nm, row, wl) in enumerate(vec_jobs):
        for ch in range(8):
            w0 = tuple(wl) + ((t_mev[11], t_bqk, t_csb) if (vi == 0 and ch == 0) else ())
            t_rt2 = p.o("tensor", "transpose", ps_row[:, vi * 8 + ch:vi * 8 + ch + 1],
                        row[:, ch * 128:(ch + 1) * 128],
                        ident_f[0:1, 0:1], waits=w0,
                        inc=pe if (vi == 4 and ch == 7) else None)
    p.o("vector", "tensor_copy", sa_f[:, :], ps_row[:, 0:8], waits=(t_rt2,))
    p.o("vector", "tensor_scalar", ca1_f[:, :], ps_row[:, 8:16], 1.0, None, ALU.add)
    p.o("vector", "tensor_scalar", cb1_f[:, :], ps_row[:, 16:24], 1.0, None, ALU.add)
    p.o("vector", "tensor_copy", sb_f[:, :], ps_row[:, 24:32])
    t_modf = p.o("vector", "tensor_copy", gb_f[:, :], ps_row[:, 32:40], inc=ve)

    bc_jobs = []
    bc_jobs.append(("ga", gaf[:, 0, :], ga_bc, None, (t_gaf[0],)))
    bc_jobs.append(("bv", rows[:, 2 * D:3 * D], bv_bc, None, (t_rows,)))
    bc_jobs.append(("bo", rows[:, 3 * D:4 * D], gabo_bc, "gabo", (t_rows,)))
    ps_free = [None, None]
    t_gabc = None
    for bi, (nm, row, outt, post, wl) in enumerate(bc_jobs):
        for half in range(2):
            slot = (2 * bi + half) % 2
            t_p = p.o("tensor", "matmul", ps_bc[:, slot, :], ones_f[:, :],
                      row[:, half * 512:half * 512 + 512], start=True, stop=True,
                      waits=tuple(wl) + ((ps_free[slot],) if ps_free[slot] else ()),
                      inc=pe)
            dst = outt[:, half * 512:half * 512 + 512]
            if post == "gabo":
                ps_free[slot] = p.o("vector", "tensor_tensor", dst, ps_bc[:, slot, :],
                                    ga_bc[:, half * 512:half * 512 + 512], ALU.mult,
                                    waits=(t_p, t_gabc), inc=ve)
            else:
                ps_free[slot] = p.o("vector", "tensor_copy", dst, ps_bc[:, slot, :],
                                    waits=(t_p,), inc=ve)
            if nm == "ga" and half == 1:
                t_gabc = ps_free[slot]
    t_setup_bc = ve.tok()

    t_nmA = [ln_stats(statsA, x_tm, 0, 8, waits=(t_xq[0], t_xq[1])),
             ln_stats(statsA, x_tm, 8, 8, waits=(t_xq[2], t_xq[3]))]
    p.run_block("setup")
    esS.close()

    esAB = ExitStack()
    q_cur = sb(esAB, "q_cur", [128, 2, NKC, 512], BF16)
    k_fm = sb(esAB, "k_fm", [128, NKC, T], BF16)
    v_tm = sb(esAB, "v_tm", [128, NT, NH, 65], BF16)

    esA = ExitStack()
    h1_half = sb(esA, "h1_half", [128, NKC, 1024], BF16)
    h1a = sb(esA, "h1a", [128, 2, D], F32)
    q_st = sb(esA, "q_st", [128, 2, 512], BF16)
    wqk_buf = sb(esA, "wqk_buf", [128, 2, NKC, 128], BF16)
    wv_buf = sb(esA, "wv_buf", [128, NKC, 512], BF16)
    ps_h1t = pt(esA, "ps_h1t", [128, 2, 1024], F32)
    ps_qkv = pt(esA, "ps_qkv", [128, 4, 512], F32)

    t_vones = p.o("gpsimd", "memset", v_tm[:, :, :, 64:65], 1.0, inc=ge)
    h1a_free = [None, None]
    qst_free = [None, None]
    psq_free = [None, None, None, None]
    psh_free = [None, None]
    wqk_free = [None, None]
    wq_r = wq_d.rearrange("(c p) m -> p c m", p=128)
    wk_r = wk_d.rearrange("(c p) m -> p c m", p=128)
    wv_r = wv_d.rearrange("(c p) m -> p c m", p=128)
    t_wv = None
    wv_free = [None, None]
    psq_i = 0

    for hf in range(2):
        r0 = hf * 8
        wq_r0 = wq_r if True else None
        pre_w = {}
        for mc in (0, 1):
            semw = d_a if mc == 0 else d_b
            pre_w[mc] = p.dma("gpsimd", wqk_buf[:, mc, :, :],
                              wq_r[:, :, mc * 128:(mc + 1) * 128], semw,
                              waits=(wqk_free[mc],))
            wqk_free[mc] = None
        t_nm = t_nmA[hf]
        t_h1ev = None
        for tl in range(8):
            t = r0 + tl
            par = tl % 2
            tA = p.o("scalar", "activation", h1a[:, par, :], x_tm[:, t, :], AF.Identity,
                     bias=statsA["negmrb"][:, t:t + 1], scale=statsA["rstdb"][:, t:t + 1],
                     waits=(t_nm, t_xq[2 * hf], t_xq[2 * hf + 1],
                            h1a_free[par]), inc=se)
            tD = None
            for ch in range(8):
                tD = p.o("tensor", "transpose", ps_h1t[:, par, ch * 128:(ch + 1) * 128],
                         h1a[:, par, ch * 128:(ch + 1) * 128], ident_f[:, :],
                         waits=(tA, psh_free[par]) if ch == 0 else (),
                         inc=pe if ch == 7 else None)
            h1a_free[par] = tD
            for ch in range(8):
                t_h1ev = p.o("vector", "tensor_scalar",
                             h1_half[:, ch, tl * 128:(tl + 1) * 128],
                             ps_h1t[:, par, ch * 128:(ch + 1) * 128],
                             ca1_f[:, ch:ch + 1], sa_f[:, ch:ch + 1],
                             ALU.mult, ALU.add,
                             waits=(tD, t_modf) if ch == 0 else (),
                             inc=ve if ch == 7 else None)
            psh_free[par] = t_h1ev

        for pi, (w_r, qk_out) in enumerate(((wq_r, None), (wk_r, k_fm))):
            for mc in range(8):
                slot = mc % 2
                semw = d_a if slot == 0 else d_b
                if pi == 0 and mc < 2:
                    t_w = pre_w[mc]
                else:
                    t_w = p.dma("gpsimd", wqk_buf[:, slot, :, :],
                                w_r[:, :, mc * 128:(mc + 1) * 128], semw,
                                waits=(wqk_free[slot],))
                wqk_free[slot] = None
                for g in range(2):
                    sl = psq_i % 4
                    psq_i += 1
                    tmm = None
                    for kc_ in range(NKC):
                        wl = (t_w, t_h1ev, psq_free[sl]) if kc_ == 0 else ()
                        tmm = p.o("tensor", "matmul", ps_qkv[:, sl, :],
                                  wqk_buf[:, slot, kc_, :],
                                  h1_half[:, kc_, g * 512:g * 512 + 512],
                                  start=(kc_ == 0), stop=(kc_ == NKC - 1),
                                  waits=wl, inc=pe if kc_ == NKC - 1 else None)
                    if pi == 0:
                        qsl = (mc * 2 + g) % 2
                        tev = p.o("vector", "tensor_scalar",
                                  q_st[:, qsl, :],
                                  ps_qkv[:, sl, :], bqk_sb[:, mc:mc + 1],
                                  None, ALU.add,
                                  waits=(tmm, t_bqk, qst_free[qsl]), inc=ve)
                        qst_free[qsl] = p.dma(
                            "sync", q_dram[:, mc, hf * 1024 + g * 512:
                                           hf * 1024 + g * 512 + 512],
                            q_st[:, qsl, :], d_h0 if qsl == 0 else d_h1,
                            waits=(tev,))
                    else:
                        tev = p.o("vector", "tensor_scalar",
                                  qk_out[:, mc, hf * 1024 + g * 512: hf * 1024 + g * 512 + 512],
                                  ps_qkv[:, sl, :], bqk_sb[:, pi * 8 + mc:pi * 8 + mc + 1],
                                  None, ALU.add, waits=(tmm, t_bqk), inc=ve)
                    psq_free[sl] = tev
                    if g == 1:
                        wqk_free[slot] = tev
        for vh in range(2):
            t_wv = p.dma("gpsimd", wv_buf[:, :, :], wv_r[:, :, vh * 512:vh * 512 + 512],
                         d_wv0 if vh == 0 else d_wv1,
                         waits=(wv_free[vh], wv_free[1 - vh]))
            for tl in range(8):
                t = r0 + tl
                sl = psq_i % 4
                psq_i += 1
                tmm = None
                for kc_ in range(NKC):
                    wl = (t_wv, t_h1ev, psq_free[sl]) if kc_ == 0 else ()
                    tmm = p.o("tensor", "matmul", ps_qkv[:, sl, :],
                              h1_half[:, kc_, tl * 128:(tl + 1) * 128],
                              wv_buf[:, kc_, :],
                              start=(kc_ == 0), stop=(kc_ == NKC - 1),
                              waits=wl, inc=pe if kc_ == NKC - 1 else None)
                tev = p.o("vector", "tensor_tensor",
                          v_tm[:, t, vh * 8:(vh + 1) * 8, 0:64],
                          ps_qkv[:, sl, :].rearrange("p (h e) -> p h e", h=8),
                          bv_bc[:, vh * 512:vh * 512 + 512].rearrange("p (h e) -> p h e", h=8),
                          ALU.add, waits=(tmm, t_setup_bc, t_vones), inc=ve)
                psq_free[sl] = tev
                if tl == 7:
                    wv_free[vh] = tev

    p.run_block("phaseA")
    esA.close()


    esB = ExitStack()
    wo_sb = sb(esB, "wo_sb", [128, NKC, D], BF16)
    P_sb = sb(esB, "P_sb", [128, 28, 512], BF16)
    stg_cfm = sb(esB, "stg_cfm", [128, 1024], BF16)
    stage = stg_cfm.ap()[0:65, :].rearrange("p (a b) -> p a b", a=2)
    ctx_fm = stg_cfm.ap().rearrange("p (c f) -> p c f", c=8)
    ctxt = sb(esB, "ctxt", [128, 4, NH, 64], BF16)
    rden = sb(esB, "rden", [128, 2, 4], F32)
    denm = sb(esB, "denm", [128, 2, 4], F32)
    ps_s = pt(esB, "ps_s", [128, 4, 512], F32)
    ps_av = pt(esB, "ps_av", [128, 2, 512], F32)
    ps_t0 = pt(esB, "ps_t0", [128, 1024], BF16)
    ps_t1 = pt(esB, "ps_t1", [128, 1024], BF16)
    ps_t = (ps_t0, ps_t1)

    t_wo = p.dma("gpsimd", wo_sb[:, :, :], wo_d.rearrange("(c p) m -> p c m", p=128), d_wo)
    t_wo2 = None
    for ch in range(NKC):
        t_wo2 = p.o("vector", "tensor_tensor", wo_sb[:, ch, :], wo_sb[:, ch, :],
                    ga_bc[:, :], ALU.mult,
                    waits=(t_wo, t_setup_bc) if ch == 0 else (),
                    inc=ve if ch == NKC - 1 else None)

    sbank_free = [None] * 4
    ppair_free = [None] * 14
    qcur_free = [None, None]
    t_qld = [None, None]
    av_free = [None, None]
    stage_free = [None, None]
    ta_free = [None, None]
    tb_free = [None, None]
    cfm_free = None
    po_free = None
    tb_last = None
    gsi = 0
    gp = 0
    norm_tok = [None] * NH
    pre_tok = [None] * 4

    pend_epi = None
    t_qld[0] = p.dma("sync", q_cur[:, 0, :, :], q_dram[:, :, 0:512], d_q0,
                     waits=(d_h0.tok(), d_h1.tok()))
    for qc in range(4):
        pre_tok[qc] = [None] * 4
        for j in range(4):
            pre_tok[qc][j] = p.o("gpsimd", "tensor_tensor", x_tm[:, qc * 4 + j, :],
                                 x_tm[:, qc * 4 + j, :], gabo_bc[:, :], ALU.add,
                                 waits=(t_setup_bc,) if (qc == 0 and j == 0) else (),
                                 inc=ge)

        exp_toks = [[None] * 8 for _ in range(NH)]
        exp_slot = [[0] * 8 for _ in range(NH)]
        av_tok = [None] * NH
        ta_tok = [None] * NH
        ev1_tok = [None] * NH

        def tb_task(j, qcb):
            nonlocal tb_last
            bank = ps_t[j % 2]
            ttb = None
            for ch in range(8):
                ttb = p.o("tensor", "transpose",
                          bank[:, ch * 128:(ch + 1) * 128],
                          ctxt[:, j, :, :].rearrange("p h e -> p (h e)")[:, ch * 128:(ch + 1) * 128],
                          ident_b[:, :],
                          waits=(last_norm, tb_free[j % 2]) if ch == 0 else (),
                          inc=pe if ch == 7 else None)
            tcf = p.o("vector", "tensor_copy", ctx_fm[:, :, :],
                      bank.ap().rearrange("p (c f) -> p c f", c=8),
                      waits=(ttb, cfm_free), inc=ve)
            tb_free[j % 2] = tcf
            ta_free[j % 2] = tcf
            if j == 3:
                tb_last = ttb
            return tcf

        def o_task(j, half, tcf, qcb):
            nonlocal cfm_free, po_free
            tom = None
            for kc_ in range(NKC):
                wl = (tcf, t_wo2, po_free, av_free[0]) if kc_ == 0 else ()
                tom = p.o("tensor", "matmul", ps_av[:, 0, :],
                          ctx_fm[:, kc_, :],
                          wo_sb[:, kc_, half * 512:half * 512 + 512],
                          start=(kc_ == 0), stop=(kc_ == NKC - 1),
                          waits=tuple(w for w in wl if w) if kc_ == 0 else (),
                          inc=pe if kc_ == NKC - 1 else None)
            if half == 1:
                cfm_free = tom
            po_free = p.o("vector", "tensor_tensor",
                          x_tm[:, qcb * 4 + j, half * 512:half * 512 + 512],
                          x_tm[:, qcb * 4 + j, half * 512:half * 512 + 512],
                          ps_av[:, 0, :], ALU.add,
                          waits=(tom, pre_tok[qcb][j]), inc=ve)
            av_free[0] = po_free

        def emit_exp(i, h, q_, tmm):
            nonlocal gp
            slot = gp % 14
            bank = 2 * (h % 2)
            texp = p.o("scalar", "activation",
                       P_sb[:, 2 * slot:2 * slot + 2, :].rearrange("p a b -> p (a b)"),
                       ps_s[:, bank:bank + 2, :].rearrange("p a b -> p (a b)"),
                       AF.Exp, scale=0.125,
                       waits=(tmm, ppair_free[slot]), inc=se)
            sbank_free[bank] = texp
            sbank_free[bank + 1] = texp
            exp_toks[h][q_] = texp
            exp_slot[h][q_] = slot
            gp += 1

        def s_half(i, half):
            h0, h1 = 2 * i, 2 * i + 1
            defer = []
            tmm = None
            for g in range(4 * half, 4 * half + 4):
                kc0 = 2 * g
                for h, kc_ in ((h0, kc0), (h0, kc0 + 1), (h1, kc0), (h1, kc0 + 1)):
                    bank = 2 * (h % 2) + kc_ % 2
                    base_p = (h % 2) * 64
                    wl = [sbank_free[bank]]
                    if i == 0 and g == 0:
                        wl.append(t_qld[qc % 2])
                    tmm = p.o("tensor", "matmul", ps_s[:, bank, :],
                              k_fm[base_p:base_p + 64, i, kc_ * 128:(kc_ + 1) * 128],
                              q_cur[base_p:base_p + 64, qc % 2, i, :],
                              start=True, stop=True,
                              waits=tuple(w for w in wl if w),
                              inc=pe if kc_ % 2 == 1 else None)
                    if kc_ % 2 == 1:
                        if g % 4 == 3:
                            defer.append((i, h, g, tmm))
                        else:
                            emit_exp(i, h, g, tmm)
            if i == NH // 2 - 1 and half == 1:
                qcur_free[qc % 2] = tmm
            return defer

        def av_half(i, half):
            h0, h1 = 2 * i, 2 * i + 1
            for q4 in range(2 * half, 2 * half + 2):
                for hp, h in ((0, h0), (1, h1)):
                    avl = None
                    for kav in range(4 * q4, 4 * q4 + 4):
                        q_ = kav // 2
                        slot = exp_slot[h][q_]
                        wl2 = [exp_toks[h][q_]]
                        if kav == 0:
                            wl2.append(av_free[hp])
                        avl = p.o("tensor", "matmul", ps_av[0:65, hp, :],
                                  v_tm[:, kav, h, :],
                                  P_sb[:, 2 * slot + kav % 2, :],
                                  start=(kav == 0), stop=(kav == 15),
                                  waits=tuple(w for w in wl2 if w),
                                  inc=pe if kav % 2 == 1 else None)
                        if kav % 2 == 1:
                            ppair_free[slot] = avl
                    if q4 == 3:
                        av_tok[h] = avl

        def ev1_pair(i):
            h0, h1 = 2 * i, 2 * i + 1
            for hp, h in ((0, h0), (1, h1)):
                ev1_tok[h] = p.o("vector", "tensor_copy", stage[:, hp, :],
                                 ps_av[0:65, hp, :],
                                 waits=(av_tok[h], stage_free[hp]), inc=ve)
                av_free[hp] = ev1_tok[h]

        def ta_pair(i):
            h0, h1 = 2 * i, 2 * i + 1
            for hp, h in ((0, h0), (1, h1)):
                tl = None
                for jj in range(4):
                    tl = p.o("tensor", "transpose",
                             ps_t[hp][:, jj * 66:jj * 66 + 65],
                             stage[0:65, hp, jj * 128:(jj + 1) * 128],
                             ident_b[0:65, 0:65],
                             waits=(ev1_tok[h], ta_free[hp]) if jj == 0 else (),
                             inc=pe if jj == 3 else None)
                ta_tok[h] = tl
                stage_free[hp] = tl

        def ev2_pair(i):
            h0, h1 = 2 * i, 2 * i + 1
            for hp, h in ((0, h0), (1, h1)):
                wl = [ta_tok[h]]
                if h == 0:
                    wl.append(tb_last)
                tav = ps_t[hp][:, 0:264].rearrange("p (a b) -> p a b", b=66)
                p.o("vector", "tensor_copy", ctxt[:, :, h, :], tav[:, :, 0:64],
                    waits=tuple(w for w in wl if w))
                tdc = p.o("vector", "tensor_copy", denm[:, hp, :],
                          tav[:, :, 64], inc=ve)
                ta_free[hp] = tdc
            p.w("vector", ta_free[1])
            t_rc = p.o("vector", "reciprocal", rden[:, :, :], denm[:, :, :], inc=ve)
            p.w("vector", t_rc)
            for hp, h in ((0, h0), (1, h1)):
                norm_tok[h] = None
                for jn in range(4):
                    norm_tok[h] = p.o("vector", "tensor_scalar", ctxt[:, jn, h, :],
                                      ctxt[:, jn, h, :], rden[:, hp, jn:jn + 1],
                                      None, ALU.mult, inc=ve if jn == 3 else None)

        NPAIR = NH // 2
        if qc + 1 < 4:
            t_qld[(qc + 1) % 2] = p.dma(
                "sync", q_cur[:, (qc + 1) % 2, :, :],
                q_dram[:, :, (qc + 1) * 512:(qc + 2) * 512],
                d_q0 if (qc + 1) % 2 == 0 else d_q1,
                waits=(qcur_free[(qc + 1) % 2],))
        for i in range(NPAIR):
            d3 = s_half(i, 0)
            if i > 0:
                av_half(i - 1, 1)
            elif pend_epi is not None:
                pend_epi()
            for args in d3:
                emit_exp(*args)
            if i > 0:
                ev1_pair(i - 1)
            d7 = s_half(i, 1)
            if i > 0:
                ta_pair(i - 1)
            av_half(i, 0)
            for args in d7:
                emit_exp(*args)
            if i > 0:
                ev2_pair(i - 1)
        av_half(NPAIR - 1, 1)
        ev1_pair(NPAIR - 1)
        ta_pair(NPAIR - 1)
        ev2_pair(NPAIR - 1)
        last_norm = norm_tok[NH - 1]

        def mk_epi(qcb, lnorm):
            def epi():
                nonlocal last_norm, pend_epi
                saved = last_norm
                last_norm = lnorm
                for jj in range(4):
                    tcf = tb_task(jj, qcb)
                    o_task(jj, 0, tcf, qcb)
                    o_task(jj, 1, tcf, qcb)
                stage_free[0] = cfm_free
                stage_free[1] = cfm_free
                last_norm = saved
                pend_epi = None
            return epi

        pend_epi = mk_epi(qc, last_norm)
        if qc == 3:
            pend_epi()

    p.run_block("phaseB")
    esB.close()
    esAB.close()

    esC = ExitStack()
    nwdT = sb(esC, "nwdT", [128, NI, D], BF16)
    G2_bc = sb(esC, "G2_bc", [128, D], F32)
    swd_all = sb(esC, "swd_all", [128, 8], F32)
    statsC = mk_stats(esC, "C")
    magic_col = sb(esC, "magic_col", [128, 1], F32)
    gmax1 = sb(esC, "gmax1", [1, 1], F32)
    gmax2 = sb(esC, "gmax2", [1, 1], F32)
    srow = sb(esC, "srow", [1, 4], F32)
    rsx1 = sb(esC, "rsx1", [128, 2], F32)
    rsx2 = sb(esC, "rsx2", [128, 2], F32)
    hred = sb(esC, "hred", [128, 24], F32)
    hacc = sb(esC, "hacc", [128, 1], F32)
    hrow = sb(esC, "hrow", [1, 128], F32)
    hl = sb(esC, "hl", [1, 1], F32)
    sig_sb = sb(esC, "sig_sb", [1, 2], F32)
    esPM = ExitStack()
    xn_fm = sb(esPM, "xn_fm", [128, NKC, T], BF16)
    ps_m = pt(esPM, "ps_m", [128, 512], F32)
    ps_r2 = pt(esPM, "ps_r2", [1, 1024], F32)
    esH = ExitStack()
    h2a = sb(esH, "h2a", [128, 2, D], F32)
    h2blk = sb(esH, "h2blk", [128, 2, D], F32)
    h2buf = sb(esH, "h2buf", [128, 2, 2 * D], F32)
    wd_nat = sb(esH, "wd_nat", [128, II], F32)
    nwd_nat = sb(esH, "nwd_nat", [128, II], BF16)
    mxw = sb(esH, "mxw", [128, 2, 2], F32)
    ps_h2 = pt(esH, "ps_h2", [128, 1024], F32)
    ps_tw = pt(esH, "ps_tw", [128, 1024], BF16)

    t_magic = p.o("gpsimd", "memset", magic_col[:, :], MAGIC, inc=ge)
    t_nm = ln_stats(statsC, x_tm, 0, 8)
    h2a_free = [None, None]
    h2blk_free = [None, None]
    psh2_free = None
    pstw_free = None
    t_swd = [None] * 8
    t_nwdT_done = None
    t_hm = None
    t_h2sp = [None] * NT
    for t in range(NT):
        if t == 8:
            t_nm = ln_stats(statsC, x_tm, 8, 8)
        par = t % 2
        tA = p.o("scalar", "activation", h2a[:, par, :], x_tm[:, t, :], AF.Identity,
                 bias=statsC["negmrb"][:, t:t + 1], scale=statsC["rstdb"][:, t:t + 1],
                 waits=(t_nm, h2a_free[par]), inc=se)
        tD = None
        for ch in range(8):
            tD = p.o("tensor", "transpose", ps_h2[:, ch * 128:(ch + 1) * 128],
                     h2a[:, par, ch * 128:(ch + 1) * 128], ident_f[:, :],
                     waits=(tA, psh2_free) if ch == 0 else (),
                     inc=pe if ch == 7 else None)
        h2a_free[par] = tD
        tev = None
        for ch in range(8):
            tev = p.o("vector", "tensor_scalar",
                      h2blk[:, par, ch * 128:(ch + 1) * 128],
                      ps_h2[:, ch * 128:(ch + 1) * 128],
                      cb1_f[:, ch:ch + 1], sb_f[:, ch:ch + 1],
                      ALU.mult, ALU.add,
                      waits=(tD, t_modf, h2blk_free[par]) if ch == 0 else (),
                      inc=ve if ch == 7 else None)
        psh2_free = tev
        p.w("vector", tev)
        t_hm = p.o("vector", "tensor_reduce", hred[:, t:t + 1], h2blk[:, par, :],
                   AX.X, ALU.max, apply_absolute_value=True, inc=ve)
        t_h2sp[t] = p.dma("sync", h2_dram[t, :, :], h2blk[:, par, :],
                          d_h0 if par == 0 else d_h1, waits=(t_hm,))
        h2blk_free[par] = t_h2sp[t]
    p.w("vector", t_hm)
    t_hm = p.o("vector", "tensor_reduce", hacc[:, :], hred[:, 0:NT], AX.X, ALU.max,
               apply_absolute_value=True, inc=ve)
    t_ht = p.o("tensor", "transpose", ps_r2[:, 0:128], hacc[:, :], ident_f[:, :],
               waits=(t_hm,), inc=pe)
    t_hr = p.o("vector", "tensor_copy", hrow[:, :], ps_r2[:, 0:128], waits=(t_ht,), inc=ve)
    p.w("vector", t_hr)
    t_hl = p.o("vector", "tensor_reduce", hl[:, :], hrow[:, :], AX.X, ALU.max, inc=ve)

    t_d1 = p.dma("gpsimd", cc1_i, hl[:, :], d_cc, waits=(t_hl,))
    t_cc1 = p.o("gpsimd", "collective_compute", "AllReduce", ALU.max,
                replica_groups=[list(range(8))], ins=[cc1_i], outs=[cc1_o],
                waits=(t_d1,), inc=cc)
    t_g1 = p.dma("gpsimd", gmax1[:, :], cc1_o, d_cc, waits=(t_cc1,))

    wd_state = {"free": None, "act": None, "nfree": None, "pstw": pstw_free,
                "done": None}

    def wd_chunk(hc):
        t_wdc = p.dma("sync", wd_nat[:, :], wd_d[hc * 128:(hc + 1) * 128, :],
                      d_wd, waits=(wd_state["free"],))
        tmx = p.o("vector", "tensor_reduce", mxw[:, 0, 0:1], wd_nat[:, :],
                  AX.X, ALU.max, apply_absolute_value=True, waits=(t_wdc,), inc=ve)
        p.w("vector", tmx)
        t_swd[hc] = p.o("vector", "tensor_scalar", swd_all[:, hc:hc + 1],
                        mxw[:, 0, 0:1], 1.0 / 127.0, None, ALU.mult, inc=ve)
        p.w("vector", t_swd[hc])
        trc = p.o("vector", "reciprocal", mxw[:, 0, 1:2], swd_all[:, hc:hc + 1],
                  waits=(wd_state["act"],), inc=ve)
        wd_state["act"] = p.o("scalar", "activation", wd_nat[:, :], wd_nat[:, :],
                              AF.Identity, scale=mxw[:, 0, 1:2],
                              bias=magic_col[:, 0:1],
                              waits=(trc, t_magic), inc=se)
        tqd = p.o("vector", "tensor_scalar", nwd_nat[:, :], wd_nat[:, :],
                  MAGIC, None, ALU.subtract,
                  waits=(wd_state["act"], wd_state["nfree"]), inc=ve)
        wd_state["free"] = tqd
        for b0 in range(0, NI, 8):
            nb = min(8, NI - b0)
            ttr = None
            for bi in range(nb):
                i = b0 + bi
                ttr = p.o("tensor", "transpose", ps_tw[:, bi * 128:(bi + 1) * 128],
                          nwd_nat[:, i * 128:(i + 1) * 128], ident_b[:, :],
                          waits=(tqd, wd_state["pstw"]) if bi == 0 else (),
                          inc=pe if bi == nb - 1 else None)
            wd_state["pstw"] = p.o("scalar", "activation",
                                   nwdT[:, b0:b0 + nb, hc * 128:(hc + 1) * 128],
                                   ps_tw.ap().rearrange("p (c f) -> p c f", c=8)[:, 0:nb, :],
                                   AF.Identity, waits=(ttr,), inc=se)
            wd_state["nfree"] = ttr
        wd_state["done"] = wd_state["pstw"]

    for hc in range(3):
        wd_chunk(hc)

    t_sg1 = p.o("scalar", "activation", sig_sb[:, 0:1], clip_sb[:, 0:1], AF.Sigmoid,
                waits=(t_rows,), inc=se)
    t_sg2 = p.o("scalar", "activation", sig_sb[:, 1:2], clip_sb[:, 1:2], AF.Sigmoid,
                inc=se)
    tq = p.o("vector", "tensor_tensor", sc_sb[:, 0:1], gmax1[:, :], sig_sb[:, 0:1],
             ALU.mult, waits=(t_g1, t_sg1), inc=ve)
    p.w("vector", tq)
    tq = p.o("vector", "reciprocal", sc_sb[:, 1:2], sc_sb[:, 0:1], inc=ve)
    p.w("vector", tq)
    p.o("vector", "tensor_scalar", srow[:, 0:1], sc_sb[:, 1:2], 127.0, None, ALU.mult)
    tq = p.o("vector", "tensor_scalar", srow[:, 1:2], sc_sb[:, 0:1], 1.0 / 127.0, None,
             ALU.mult, inc=ve)
    p.w("vector", tq)
    t_bq = p.o("tensor", "matmul", ps_m[:, 0:2], ones_f[:, :], srow[0:1, 0:2],
               start=True, stop=True, waits=(tq,), inc=pe)
    t_rsx1 = p.o("vector", "tensor_copy", rsx1[:, :], ps_m[:, 0:2], waits=(t_bq,), inc=ve)
    p.w("vector", t_rsx1)

    h2r = h2_dram.rearrange("t p (c f) -> p c t f", c=8)
    t_spall = (d_h0, d_h0.val)
    t_spall1 = (d_h1, d_h1.val)
    h2b_free = [None, None]
    t_xn_par = [None, None]
    t_ld = [None, None]
    t_ld[0] = p.dma("sync", h2buf[:, 0, :].rearrange("p (t f) -> p t f", t=NT),
                    h2r[:, 0, :, :], d_h0, waits=(t_spall, t_spall1))
    t_ld[1] = p.dma("sync", h2buf[:, 1, :].rearrange("p (t f) -> p t f", t=NT),
                    h2r[:, 1, :, :], d_h1)
    t_xnev = None
    for c0 in range(0, 8, 2):
        if c0 >= 2:
            t_ld[0] = p.dma("sync", h2buf[:, 0, :].rearrange("p (t f) -> p t f", t=NT),
                            h2r[:, c0, :, :], d_h0, waits=(h2b_free[0],))
            t_ld[1] = p.dma("sync", h2buf[:, 1, :].rearrange("p (t f) -> p t f", t=NT),
                            h2r[:, c0 + 1, :, :], d_h1, waits=(h2b_free[1],))
        for step in range(3):
            tstep = None
            for par in range(2):
                if step == 0:
                    tstep = p.o("vector", "tensor_scalar", h2buf[:, par, :],
                                h2buf[:, par, :], rsx1[:, 0:1], MAGIC, ALU.mult,
                                ALU.add, waits=(t_ld[par],), inc=ve if par else None)
                elif step == 1:
                    tstep = p.o("vector", "tensor_scalar", h2buf[:, par, :],
                                h2buf[:, par, :], MAGIC, 127.0, ALU.subtract, ALU.min,
                                inc=ve if par else None)
                else:
                    t_xnev = p.o("vector", "tensor_scalar", xn_fm[:, c0 + par, :],
                                 h2buf[:, par, :], -128.0, None, ALU.max, inc=ve)
                    h2b_free[par] = t_xnev
                    t_xn_par[par] = t_xnev
            if step < 2:
                p.w("vector", tstep)
        wd_chunk(3 + c0 // 2)
    wd_chunk(7)
    t_nwdT_done = wd_state["done"]
    esH.close()

    esL = ExitStack()
    rowC = sb(esL, "rowC", [1, D], F32)
    ps_t = pt(esL, "ps_t", [128, 1024], BF16)
    ps_g = pt(esL, "ps_g", [128, 2, 512], F32)
    ps_u = pt(esL, "ps_u", [128, 2, 512], F32)
    wg_nat = sb(esL, "wg_nat", [128, 2, D], F32)
    wu_nat = sb(esL, "wu_nat", [128, 2, D], F32)
    nw_nat = sb(esL, "nw_nat", [128, 2, 2, D], BF16)
    nwgT = sb(esL, "nwgT", [128, 2, NKC, 128], BF16)
    nwuT = sb(esL, "nwuT", [128, 2, NKC, 128], BF16)
    silu_sb = sb(esL, "silu_sb", [128, 2, 512], F32)
    v_sl = sb(esL, "v_sl", [128, T], F16)
    mxs = sb(esL, "mxs", [128, 4, 8], F32)

    wgn_free = [None, None]
    nwT_free = [[None, None], [None, None]]
    nwn_free = [[None, None], [None, None]]
    psg_free = [None, None]
    psu_free = [None, None]
    silu_free = [None, None]
    t_spill = None
    t_vmax = None
    t_sil_par = [None, None]
    t_v4_par = [None, None]
    pst_free = None
    t_f = [None, None]
    t_q = [[None, None], [None, None]]
    t_T = [[None, None], [None, None]]
    t_sv = [None, None]

    tg0 = p.dma("sync", wg_nat[:, 0, :], wg_d[0:128, :], d_f0,
                waits=(t_h2sp[NT - 2], t_h2sp[NT - 1], h2a_free[0], h2a_free[1]))
    tu0 = p.dma("sync", wu_nat[:, 0, :], wu_d[0:128, :], d_u0)
    t_f[0] = (tg0, tu0)
    tg1 = p.dma("sync", wg_nat[:, 1, :], wg_d[128:256, :], d_f1)
    tu1 = p.dma("sync", wu_nat[:, 1, :], wu_d[128:256, :], d_u1,
                waits=(t_xn_par[0],))
    t_f[1] = (tg1, tu1)

    def qd_stage(ic):
        par = ic % 2
        ql = ic % 4
        if ic >= 2:
            tgx = p.dma("sync", wg_nat[:, par, :], wg_d[ic * 128:(ic + 1) * 128, :],
                        d_f0 if par == 0 else d_f1, waits=(wgn_free[par],))
            tux = p.dma("sync", wu_nat[:, par, :], wu_d[ic * 128:(ic + 1) * 128, :],
                        d_u0 if par == 0 else d_u1)
            t_f[par] = (tgx, tux)
        wnats = (wg_nat, wu_nat)
        tl = None
        for wi in range(2):
            wl0 = (t_f[par] + (wd_state["nfree"],)) if (ic < 2 and wi == 0) else \
                  (t_f[par] if wi == 0 else ())
            tl = p.o("vector", "tensor_reduce", mxs[:, ql, wi:wi + 1],
                     wnats[wi][:, par, :], AX.X, ALU.max, apply_absolute_value=True,
                     waits=wl0, inc=ve if wi else None)
        p.w("vector", tl)
        for wi in range(2):
            tl = p.o("vector", "tensor_scalar", mxs[:, ql, wi:wi + 1],
                     mxs[:, ql, wi:wi + 1], 1.0 / 127.0, None, ALU.mult,
                     inc=ve if wi else None)
        p.w("vector", tl)
        for wi in range(2):
            tl = p.o("vector", "reciprocal", mxs[:, ql, 2 + wi:3 + wi],
                     mxs[:, ql, wi:wi + 1], inc=ve if wi else None)
        t_sv[par] = p.o("vector", "tensor_scalar", mxs[:, ql, 4:6],
                        mxs[:, ql, 0:2], rsx1[:, 1:2], None, ALU.mult,
                        waits=(t_sil_par[par], t_v4_par[par]), inc=ve)
        p.w("vector", t_sv[par])
        for wi in range(2):
            tl = p.o("vector", "tensor_scalar", wnats[wi][:, par, :],
                     wnats[wi][:, par, :], mxs[:, ql, 2 + wi:3 + wi], MAGIC,
                     ALU.mult, ALU.add, inc=ve if wi else None)
        p.w("vector", tl)
        for wi in range(2):
            t_q[par][wi] = p.o("vector", "tensor_scalar", nw_nat[:, par, wi, :],
                               wnats[wi][:, par, :], MAGIC, None, ALU.subtract,
                               waits=(nwn_free[par][wi],), inc=ve)
        wgn_free[par] = t_q[par][1]

    def qt_stage(ic):
        nonlocal pst_free
        par = ic % 2
        for wi, nwT in enumerate((nwgT, nwuT)):
            ttr = None
            for ch in range(8):
                ttr = p.o("tensor", "transpose", ps_t[:, ch * 128:(ch + 1) * 128],
                          nw_nat[:, par, wi, ch * 128:(ch + 1) * 128], ident_b[:, :],
                          waits=(t_q[par][wi], pst_free) if ch == 0 else (),
                          inc=pe if ch == 7 else None)
            nwn_free[par][wi] = ttr
            t_T[par][wi] = p.o("scalar", "activation", nwT[:, par, :, :],
                               ps_t.ap().rearrange("p (c f) -> p c f", c=8),
                               AF.Identity, waits=(ttr, nwT_free[par][wi]), inc=se)
            pst_free = t_T[par][wi]

    def mm_stage(ic):
        nonlocal t_spill, t_vmax
        par = ic % 2
        ql = ic % 4
        tg = tu = t_v4 = tsil = None
        for tc in range(4):
            gsl = tc % 2
            for kc_ in range(NKC):
                wl = (t_T[par][0], psg_free[gsl],
                      t_xnev if (ic == 0 and tc == 0) else None) if kc_ == 0 else ()
                tg = p.o("tensor", "matmul", ps_g[:, gsl, :],
                         nwgT[:, par, kc_, :], xn_fm[:, kc_, tc * 512:tc * 512 + 512],
                         start=(kc_ == 0), stop=(kc_ == NKC - 1),
                         waits=tuple(w for w in wl if w) if kc_ == 0 else (),
                         inc=pe if kc_ == NKC - 1 else None)
            tsil = p.o("scalar", "activation", silu_sb[:, gsl, :], ps_g[:, gsl, :],
                       AF.Silu, scale=mxs[:, ql, 4:5],
                       waits=(tg, t_sv[par], silu_free[gsl]), inc=se)
            psg_free[gsl] = tsil
            for kc_ in range(NKC):
                wl = (t_T[par][1], psu_free[gsl]) if kc_ == 0 else ()
                tu = p.o("tensor", "matmul", ps_u[:, gsl, :],
                         nwuT[:, par, kc_, :], xn_fm[:, kc_, tc * 512:tc * 512 + 512],
                         start=(kc_ == 0), stop=(kc_ == NKC - 1),
                         waits=wl, inc=pe if kc_ == NKC - 1 else None)
            t_v4 = p.o("vector", "scalar_tensor_tensor",
                       v_sl[:, tc * 512:tc * 512 + 512],
                       ps_u[:, gsl, :], mxs[:, ql, 5:6], silu_sb[:, gsl, :],
                       ALU.mult, ALU.mult,
                       waits=(tu, tsil, t_spill if tc == 0 else None,
                              t_vmax if tc == 0 else None), inc=ve)
            psu_free[gsl] = t_v4
            silu_free[gsl] = t_v4
        nwT_free[par] = [tg, tu]
        t_sil_par[par] = tsil
        t_v4_par[par] = t_v4
        t_vmax = p.o("vector", "tensor_reduce", hred[:, ic:ic + 1], v_sl[:, :],
                     AX.X, ALU.max, apply_absolute_value=True, waits=(t_v4,), inc=ve)
        t_spill = p.dma("sync", v_dram[ic * 128:(ic + 1) * 128, :], v_sl[:, :], d_sp,
                        waits=(t_v4,))

    qd_stage(0)
    qd_stage(1)
    qt_stage(0)
    for ic in range(NI):
        if ic + 2 < NI:
            qd_stage(ic + 2)
        mm_stage(ic)
        if ic + 1 < NI:
            qt_stage(ic + 1)

    p.w("vector", t_vmax)
    t_hm2 = p.o("vector", "tensor_reduce", hacc[:, :], hred[:, 0:NI], AX.X, ALU.max,
                apply_absolute_value=True, inc=ve)
    t_ht2 = p.o("tensor", "transpose", ps_r2[:, 0:128], hacc[:, :], ident_f[:, :],
                waits=(t_hm2,), inc=pe)
    t_hr2 = p.o("vector", "tensor_copy", hrow[:, :], ps_r2[:, 0:128], waits=(t_ht2,), inc=ve)
    p.w("vector", t_hr2)
    t_hl2 = p.o("vector", "tensor_reduce", hl[:, :], hrow[:, :], AX.X, ALU.max, inc=ve)
    t_d2 = p.dma("gpsimd", cc2_i, hl[:, :], d_cc, waits=(t_hl2,))
    t_cc2 = p.o("gpsimd", "collective_compute", "AllReduce", ALU.max,
                replica_groups=[list(range(8))], ins=[cc2_i], outs=[cc2_o],
                waits=(t_d2,), inc=cc)
    t_g2 = p.dma("gpsimd", gmax2[:, :], cc2_o, d_cc, waits=(t_cc2,))
    tq = p.o("vector", "tensor_tensor", sc_sb[:, 2:3], gmax2[:, :], sig_sb[:, 1:2],
             ALU.mult, waits=(t_g2, t_sg2), inc=ve)
    p.w("vector", tq)
    tq = p.o("vector", "reciprocal", sc_sb[:, 3:4], sc_sb[:, 2:3], inc=ve)
    p.w("vector", tq)
    p.o("vector", "tensor_scalar", srow[:, 2:3], sc_sb[:, 3:4], 127.0, None, ALU.mult)
    tq = p.o("vector", "tensor_scalar", srow[:, 3:4], sc_sb[:, 2:3], 1.0 / 127.0,
             None, ALU.mult, inc=ve)
    p.w("vector", tq)
    t_bq2 = p.o("tensor", "matmul", ps_m[:, 0:2], ones_f[:, :], srow[0:1, 2:4],
                start=True, stop=True, waits=(tq,), inc=pe)
    t_rsx2 = p.o("vector", "tensor_copy", rsx2[:, :], ps_m[:, 0:2], waits=(t_bq2,), inc=ve)
    p.w("vector", t_rsx2)
    t_swdg = p.o("vector", "tensor_tensor", swd_all[:, :], swd_all[:, :],
                 gb_f[:, :], ALU.mult, waits=(t_swd[7],), inc=ve)
    p.w("vector", t_swdg)
    tsr = None
    for hc in range(8):
        tsr = p.o("tensor", "transpose", ps_r2[:, hc * 128:(hc + 1) * 128],
                  swd_all[:, hc:hc + 1], ident_f[:, :],
                  waits=(t_swdg,) if hc == 0 else (),
                  inc=pe if hc == 7 else None)
    t_row = p.o("vector", "tensor_copy", rowC[:, :], ps_r2[:, 0:1024], waits=(tsr,), inc=ve)
    p.w("vector", t_row)
    t_G2 = None
    for half in range(2):
        tbb = p.o("tensor", "matmul", ps_m[:, :], ones_f[:, :],
                  rowC[0:1, half * 512:half * 512 + 512], start=True, stop=True,
                  waits=(t_row,) if half == 0 else (t_G2,), inc=pe)
        t_G2 = p.o("vector", "tensor_scalar", G2_bc[:, half * 512:half * 512 + 512],
                   ps_m[:, :], rsx2[:, 1:2], None,
                   ALU.mult, waits=(tbb,), inc=ve)

    p.run_block("phaseC")
    esL.close()
    esPM.close()

    esD = ExitStack()
    v_tc = sb(esD, "v_tc", [128, 2, NI, 512], F16)
    n2 = sb(esD, "n2", [128, 2, 512], BF16)
    tq1 = sb(esD, "tq1", [128, 2, 512], F32)
    tmpe = sb(esD, "tmpe", [128, 4, 512], F32)
    ps_d = pt(esD, "ps_d", [128, 8, 512], F32)

    v_r = v_dram.rearrange("(c p) t -> p c t", p=128)
    t_vld = [None] * 4
    vtc_free = [None, None]
    t_spill_all = d_sp.tok()

    def v_load(tc, waits=()):
        s0, s1 = (d_v0, d_v1) if tc % 2 == 0 else (d_f0, d_f1)
        ta_ = p.dma("sync", v_tc[:, tc % 2, 0:11, :],
                    v_r[:, 0:11, tc * 512:tc * 512 + 512], s0, waits=waits)
        tb_ = p.dma("sync", v_tc[:, tc % 2, 11:NI, :],
                    v_r[:, 11:NI, tc * 512:tc * 512 + 512], s1, waits=waits)
        return (ta_, tb_)

    t_vld[0] = v_load(0, waits=(t_spill_all,))
    t_vld[1] = v_load(1)

    n2_free = [None, None]
    tq_free = [None, None]
    tmpe_free = [None] * 4
    t_out = None
    def quant_pair(tc_, ic0):
        par_ = tc_ % 2
        t_s0 = [None, None]
        for npar in range(2):
            ic = ic0 + npar
            t_s0[npar] = p.o("scalar", "activation", tq1[:, npar, :],
                             v_tc[:, par_, ic, :], AF.Identity,
                             scale=rsx2[:, 0:1], bias=magic_col[:, 0:1],
                             waits=t_vld[tc_] + (t_rsx2, t_magic, tq_free[npar])
                             if ic < 2 else (tq_free[npar],), inc=se)
        res = [None, None]
        for step in range(2):
            tstep = None
            for npar in range(2):
                if step == 0:
                    tstep = p.o("vector", "tensor_scalar", tq1[:, npar, :],
                                tq1[:, npar, :], MAGIC, 127.0,
                                ALU.subtract, ALU.min,
                                waits=(t_s0[npar],), inc=ve if npar else None)
                else:
                    res[npar] = p.o("vector", "tensor_scalar", n2[:, npar, :],
                                    tq1[:, npar, :], -128.0, None, ALU.max,
                                    waits=(n2_free[npar],), inc=ve)
                    tq_free[npar] = res[npar]
            if step < 1:
                p.w("vector", tstep)
        return res

    pre_n2 = quant_pair(0, 0)
    for tc in range(4):
        par = tc % 2
        mm_tok = [None] * 8
        t_icmm = None
        for ic0 in range(0, NI, 2):
            t_n2p = pre_n2 if ic0 == 0 else quant_pair(tc, ic0)
            for npar in range(2):
                ic = ic0 + npar
                t_n2 = t_n2p[npar]
                for tt in range(4):
                    for hh in range(2):
                        k = tt * 2 + hh
                        wl = []
                        if ic == 0:
                            wl = [t_n2, t_nwdT_done]
                            if tc >= 1:
                                wl.append(mm_free[k])
                        elif tt == 0 and hh == 0:
                            wl = [t_n2]
                        t_icmm = p.o("tensor", "matmul", ps_d[:, k, :],
                                     n2[:, npar, tt * 128:(tt + 1) * 128],
                                     nwdT[:, ic, hh * 512:hh * 512 + 512],
                                     start=(ic == 0), stop=(ic == NI - 1),
                                     waits=tuple(wl),
                                     inc=pe if (ic == NI - 1 or (tt == 3 and hh == 1)) else None)
                        if ic == NI - 1:
                            mm_tok[k] = t_icmm
                n2_free[npar] = t_icmm
                if ic == NI - 1:
                    vtc_free[par] = t_icmm
        if tc + 2 < 4:
            t_vld[tc + 2] = v_load(tc + 2, waits=(vtc_free[par],))
        if tc + 1 < 4:
            pre_n2 = quant_pair(tc + 1, 0)
        mm_free = [None] * 8
        txa_tok = [None] * 8
        for grp in range(2):
            tmu_last = None
            for kk in range(4):
                k = grp * 4 + kk
                tt, hh = k // 2, k % 2
                tmu_last = p.o("vector", "tensor_tensor", tmpe[:, kk, :], ps_d[:, k, :],
                               G2_bc[:, hh * 512:hh * 512 + 512], ALU.mult,
                               waits=(mm_tok[k], t_G2, tmpe_free[kk]), inc=ve)
                mm_free[k] = tmu_last
            p.w("vector", tmu_last)
            txa_last = None
            for kk in range(4):
                k = grp * 4 + kk
                tt, hh = k // 2, k % 2
                txa_last = p.o("vector", "tensor_tensor",
                               x_tm[:, tc * 4 + tt, hh * 512:hh * 512 + 512],
                               x_tm[:, tc * 4 + tt, hh * 512:hh * 512 + 512],
                               tmpe[:, kk, :], ALU.add, inc=ve)
                tmpe_free[kk] = txa_last
                txa_tok[k] = txa_last
            p.w("vector", txa_last)
        for tt in range(4):
            is_last = (tc == 3 and tt == 3)
            t_out = p.dma("sync",
                          out_d.rearrange("(n p) d -> p n d", p=128)[:, tc * 4 + tt, :],
                          x_tm[:, tc * 4 + tt, :], d_out,
                          waits=(txa_tok[tt * 2 + 1],) + (((d_out, 240),) if is_last else ()))
    p.w("sync", d_out.tok())
    p.run_block("phaseD")
    esD.close()
    esC.close()
    es0.close()
    return nc


_NC_CACHE = None


def kernel(**inputs):
    global _NC_CACHE
    f32 = lambda k: np.ascontiguousarray(np.asarray(inputs[k], dtype=np.float32))
    x = f32("L0_attn_x")
    c = f32("L0_c")
    b_rows_base = np.stack([
        f32("L0_attn_bq").reshape(D), f32("L0_attn_bk").reshape(D),
        f32("L0_attn_bv").reshape(D), f32("L0_attn_bo").reshape(D),
        np.zeros(D, np.float32)], axis=0)
    clips = np.stack([f32("L0_ffn_clip_x").reshape(()),
                      f32("L0_ffn_clip_down").reshape(())]).reshape(1, 2)
    common = {
        "wq": f32("L0_attn_Wq"), "wk": f32("L0_attn_Wk"),
        "wv": f32("L0_attn_Wv"), "wo": f32("L0_attn_Wo"),
        "wg": f32("L0_ffn_w_gate"), "wu": f32("L0_ffn_w_up"),
        "wd": f32("L0_ffn_w_down"), "wm": f32("L0_wmod"),
        "clips": np.ascontiguousarray(clips),
    }
    if _NC_CACHE is None:
        _NC_CACHE = build_nc()
    nc = _NC_CACHE
    in_maps = []
    for b in range(8):
        rows_b = b_rows_base.copy()
        rows_b[4] = c[b]
        m = dict(common)
        m["x_in"] = np.ascontiguousarray(x[b])
        m["b_rows"] = np.ascontiguousarray(rows_b.reshape(1, 5 * D))
        in_maps.append(m)
    res = run_bass_kernel_spmd(nc, in_maps, core_ids=list(range(8)))
    out = np.stack([np.asarray(r["out"], dtype=np.float32) for r in res.results], axis=0)
    return out
